# version 15
# baseline (speedup 1.0000x reference)
import contextlib

import numpy as np
import ml_dtypes

import concourse.bass as bass
import concourse.mybir as mybir
from concourse.bass_utils import run_bass_kernel_spmd

F32 = mybir.dt.float32
BF16 = mybir.dt.bfloat16
AF = mybir.ActivationFunctionType
ALU = mybir.AluOpType
AX = mybir.AxisListType

D = 1024
H = 4096
E = 8
T = 2048
KD = D // 128
TT = T // 128
NT = T // 512
NH = 16
ND = 8
HALF = 2048


class Sched:

    ENGINES = ("tensor", "scalar", "vector", "gpsimd", "sync")

    def __init__(self):
        self.ops = {e: [] for e in self.ENGINES}
        self.cnt = {}
        self.last_wait = {e: {} for e in self.ENGINES}

    def op(self, eng, fn, waits=(), inc=None, inc_k=1):
        w = []
        for sem, val in waits:
            if val <= 0:
                continue
            if self.last_wait[eng].get(sem, -1) >= val:
                continue
            self.last_wait[eng][sem] = val
            w.append((sem, val))
        if inc is not None:
            self.cnt[inc] = self.cnt.get(inc, 0) + inc_k
        self.ops[eng].append((fn, w, inc, inc_k))
        return self.cnt.get(inc, 0)

    def c(self, sem):
        return self.cnt.get(sem, 0)

    def emit(self, block, sems):
        def mk(eng_name):
            def body(e):
                for fn, waits, inc, inc_k in self.ops[eng_name]:
                    for sem, val in waits:
                        e.wait_ge(sems[sem], val)
                    instr = fn(e)
                    if inc is not None:
                        instr.then_inc(sems[inc], inc_k)
            return body
        for eng_name in self.ENGINES:
            if self.ops[eng_name]:
                getattr(block, eng_name)(mk(eng_name))


def build_program(nexp=8, debug=False):
    nc = bass.Bass("TRN2", target_bir_lowering=False)

    dt = lambda n, s: nc.dram_tensor(n, s, F32, kind="ExternalInput").ap()
    X = dt("x", [T, D])
    AWG = dt("awg", [D, D]); AWV = dt("awv", [D, D]); AWD = dt("awd", [D, D])
    BATTN = dt("battn", [3, 128, KD])
    WMOET = dt("wmoet", [D, E])
    BGR1 = dt("bgr1", [1, E]); BGR2 = dt("bgr2", [1, E])
    W1T = dt("w1t", [E, D, H])
    B1P = dt("b1p", [E, 128, 32])
    W2T = dt("w2t", [E, H, D])
    B2N = dt("b2n", [E, D])
    WS1T = dt("ws1t", [D, H])
    BS1P = dt("bs1p", [128, 32])
    WS2T = dt("ws2t", [H, D])
    BS2P = dt("bs2p", [128, KD])
    IDENTF = nc.dram_tensor("identf", [128, 128], F32, kind="ExternalInput").ap()
    ONESB = nc.dram_tensor("onesb", [1, 128], BF16, kind="ExternalInput").ap()
    ONESF = nc.dram_tensor("onesf", [1, 128], F32, kind="ExternalInput").ap()
    OUT = nc.dram_tensor("out", [T, D], F32, kind="ExternalOutput").ap()

    s = Sched()
    ctx = contextlib.ExitStack()
    sb = lambda n, shape, dty: ctx.enter_context(nc.sbuf_tensor(n, shape, dty))

    xnT = sb("xnT", [128, KD * T], BF16)
    acc = sb("acc", [128, KD * T], F32)
    w1buf = sb("w1buf", [128, KD * HALF], BF16)
    w2buf = sb("w2buf", [128, NH * D], BF16)
    wT = sb("wT", [E, T], BF16)
    b2sb = sb("b2sb", [E, D], BF16)
    wmoe_sb = sb("wmoe_sb", [128, KD * E], BF16)
    b1all = sb("b1all", [128, E * 32 + 32], F32)
    battn_sb = sb("battn_sb", [128, 3 * KD], F32)
    bs2_sb = sb("bs2_sb", [128, KD], F32)
    bgr1_sb = sb("bgr1_sb", [1, E], F32)
    bgr2_sb = sb("bgr2_sb", [1, E], F32)
    bgrow = sb("bgrow", [1, E], F32)
    bgate_bc = sb("bgate_bc", [128, E], F32)
    identf_sb = sb("identf_sb", [128, 128], F32)
    onesb_sb = sb("onesb_sb", [1, 128], BF16)
    onesf_sb = sb("onesf_sb", [1, 128], F32)
    eps_sb = sb("eps_sb", [128, 1], F32)
    stats = sb("stats", [128, 6 * TT], F32)
    wrow0 = sb("wrow0", [1, T], BF16)
    wb = sb("wb", [128, T], BF16)

    p = [ctx.enter_context(nc.psum_tensor(f"p{i}", [128, 512], F32)) for i in range(6)]
    ptF = ctx.enter_context(nc.psum_tensor("ptF", [128, 1024], F32))
    PTS = (0, 512)

    sem_names = ["cst", "xl0", "xl1", "xl2", "xl3", "wr",
                 "fx0", "fx1", "ou0", "ou1", "ou2", "ou3",
                 "wdc", "wda", "w1d", "w2d", "pe", "act", "dve",
                 "w1b4", "w1b5", "w1b6", "w1b7"]
    sems = {n: ctx.enter_context(nc.semaphore(n)) for n in sem_names}

    bank_rel = {i: ("pe", 0) for i in range(6)}
    ptF_rel = {0: ("pe", 0), 1: ("pe", 0)}

    def syncdma(dst_ap_fn, src_ap_fn, sem="cst", waits=(), eng="scalar"):
        return s.op(eng, lambda e: e.dma_start(out=dst_ap_fn(), in_=src_ap_fn()),
                    waits=waits, inc=sem, inc_k=16)

    syncdma(lambda: identf_sb[:], lambda: IDENTF[:, :])
    syncdma(lambda: onesb_sb[:], lambda: ONESB[:, :])
    syncdma(lambda: onesf_sb[:], lambda: ONESF[:, :])
    syncdma(lambda: battn_sb[:].rearrange("p (j c) -> p j c", c=KD),
            lambda: BATTN.rearrange("j p c -> p j c"))
    syncdma(lambda: b1all[:, 0:E * 32].rearrange("p (e c) -> p e c", c=32),
            lambda: B1P.rearrange("e p c -> p e c"))
    syncdma(lambda: b1all[:, E * 32:E * 32 + 32], lambda: BS1P[:, :])
    syncdma(lambda: bs2_sb[:], lambda: BS2P[:, :])
    syncdma(lambda: bgr1_sb[:], lambda: BGR1[:, :])
    syncdma(lambda: bgr2_sb[:], lambda: BGR2[:, :])
    consts_xd = s.c("cst")

    def castdma(dst_ap_fn, src_ap_fn, waits=(), sem="wdc"):
        return s.op("gpsimd", lambda e: e.dma_start(out=dst_ap_fn(), in_=src_ap_fn()),
                    waits=waits, inc=sem, inc_k=16)

    castdma(lambda: b2sb[:], lambda: B2N[:, :])
    castdma(lambda: wmoe_sb[:].rearrange("p (k e) -> p k e", e=E),
            lambda: WMOET.rearrange("(k p) e -> p k e", p=128))
    wmoe_wdc = s.c("wdc")
    castdma(lambda: w2buf[:, 0:KD * D].rearrange("p (k e) -> p k e", e=D),
            lambda: AWG.rearrange("(k p) e -> p k e", p=128), sem="wda")
    castdma(lambda: w2buf[:, KD * D:2 * KD * D].rearrange("p (k e) -> p k e", e=D),
            lambda: AWV.rearrange("(k p) e -> p k e", p=128), sem="wda")

    ctx_sc = contextlib.ExitStack()
    sc_tm = ctx_sc.enter_context(nc.sbuf_tensor("sc_tm", [128, E * TT], F32))
    sig_tm = ctx_sc.enter_context(nc.sbuf_tensor("sig_tm", [128, E * TT], F32))
    srt_tm = ctx_sc.enter_context(nc.sbuf_tensor("srt_tm", [128, E * TT], F32))
    negm2_t = ctx_sc.enter_context(nc.sbuf_tensor("negm2_t", [128, TT], F32))
    ctx_ln = contextlib.ExitStack()
    xbuf = [ctx_ln.enter_context(nc.sbuf_tensor(f"xbuf{i}", [128, D], F32)) for i in range(4)]
    xnb = [ctx_ln.enter_context(nc.sbuf_tensor(f"xnb{i}", [128, D], F32)) for i in range(2)]
    g_t = ctx_ln.enter_context(nc.sbuf_tensor("g_t", [128, D], F32))

    s.op("vector", lambda e: e.memset(eps_sb[:], 1e-5), inc="dve")

    c_bg = s.op("vector", lambda e: e.tensor_add(bgrow[:], bgr1_sb[:], bgr2_sb[:]),
                waits=[("cst", consts_xd)], inc="dve")
    c_bgmm = s.op("tensor",
                  lambda e: e.matmul(p[4][:, 0:E], onesf_sb[:], bgrow[:],
                                     start=True, stop=True),
                  waits=[("dve", c_bg), bank_rel[4]], inc="pe")
    c_bgc = s.op("vector", lambda e: e.tensor_copy(bgate_bc[:], p[4][:, 0:E]),
                 waits=[("pe", c_bgmm)], inc="dve")
    bank_rel[4] = ("dve", c_bgc)

    xl_cnt = {}
    c_red = {}; c_sqr = {}; c_vn = {}; c_ex = {}; c_b2 = {}; c_id = {}
    cp_done = {}; sc_add = {}
    sc_slot_rel = {i: ("dve", 0) for i in range(6)}
    sc_slot_rel[4] = ("dve", c_bgc)
    xrel = {}
    xnb_rel = {0: ("pe", 0), 1: ("pe", 0)}

    def st(col, i):
        return stats[:, col * TT + i:col * TT + i + 1]

    def ln_stage_a(i):
        sl4 = i % 4
        w = [xrel[sl4]] if sl4 in xrel else []
        xl_cnt[i] = s.op("sync",
                         (lambda i_, sl_: lambda e: e.dma_start(
                             out=xbuf[sl_][:], in_=X[128 * i_:128 * (i_ + 1), :]))(i, sl4),
                         waits=w, inc=f"xl{sl4}", inc_k=16)
        c_red[i] = s.op("vector",
                        (lambda i_, sl_: lambda e: e.reduce_sum(
                            st(0, i_), xbuf[sl_][:], axis=AX.X))(i, sl4),
                        waits=[(f"xl{i % 4}", xl_cnt[i])], inc="dve")
        c_sqr[i] = s.op("scalar",
                        (lambda i_, sl_: lambda e: e.activation(
                            g_t[:], xbuf[sl_][:], AF.Square,
                            accum_out=st(2, i_)))(i, sl4),
                        waits=[(f"xl{i % 4}", xl_cnt[i]),
                               ("act", c_sqr.get(i - 1, 0))], inc="act")

    def ln_stage_b1(i):
        c_m = s.op("vector",
                   (lambda i_: lambda e: e.tensor_mul(st(1, i_), st(0, i_), st(0, i_)))(i),
                   waits=[("dve", c_red[i])], inc="dve")
        c_vn[i] = s.op("vector",
                       (lambda i_: lambda e: e.scalar_tensor_tensor(
                           st(3, i_), st(1, i_), -1.0 / D, st(2, i_),
                           ALU.mult, ALU.add))(i),
                       waits=[("dve", c_m), ("act", c_sqr[i])], inc="dve")

    def ln_stage_b2(i):
        c_l = s.op("scalar",
                   (lambda i_: lambda e: e.activation(
                       st(4, i_), st(3, i_), AF.Ln,
                       bias=eps_sb[:, 0:1], scale=1.0 / D))(i),
                   waits=[("dve", c_vn[i])], inc="act")
        c_ex[i] = s.op("scalar",
                       (lambda i_: lambda e: e.activation(
                           st(4, i_), st(4, i_), AF.Exp, scale=-0.5))(i),
                       waits=[("act", c_l)], inc="act")

    def ln_stage_b3(i):
        sl4 = i % 4
        sl2 = i % 2
        c_b2[i] = s.op("vector",
                       (lambda i_: lambda e: e.scalar_tensor_tensor(
                           st(5, i_), st(0, i_), -1.0 / D, st(4, i_),
                           ALU.mult, ALU.mult))(i),
                       waits=[("act", c_ex[i])], inc="dve")
        c_id[i] = s.op("scalar",
                       (lambda i_, sl_, sl2_: lambda e: e.activation(
                           xnb[sl2_][:], xbuf[sl_][:], AF.Identity,
                           bias=st(5, i_), scale=st(4, i_)))(i, sl4, sl2),
                       waits=[("dve", c_b2[i]), xnb_rel[sl2]], inc="act")
        xrel[sl4] = ("act", c_id[i])

    def ln_stage_c(i):
        sl2 = i % 2
        last_tp = 0
        last_cp = 0
        for c in range(KD):
            tsl = (i * KD + c) % 2
            last_tp = s.op("tensor",
                           (lambda sl2_, c_, tsl_: lambda e: e.transpose(
                               ptF[:, PTS[tsl_]:PTS[tsl_] + 128],
                               xnb[sl2_][:, 128 * c_:128 * (c_ + 1)],
                               identf_sb[:]))(sl2, c, tsl),
                           waits=[("act", c_id[i]), ptF_rel[tsl], ("cst", consts_xd)],
                           inc="pe")
            last_cp = s.op("vector",
                           (lambda i_, c_, tsl_: lambda e: e.tensor_copy(
                               xnT[:, T * c_ + 128 * i_:T * c_ + 128 * (i_ + 1)],
                               ptF[:, PTS[tsl_]:PTS[tsl_] + 128]))(i, c, tsl),
                           waits=[("pe", last_tp)], inc="dve")
            ptF_rel[tsl] = ("dve", last_cp)
        xnb_rel[sl2] = ("pe", last_tp)
        cp_done[i] = last_cp
        slot = i % 6
        c_mm = 0
        for k in range(KD):
            c_mm = s.op("tensor",
                        (lambda i_, k_, slot_: lambda e: e.matmul(
                            p[slot_][:, 0:E],
                            xnT[:, T * k_ + 128 * i_:T * k_ + 128 * (i_ + 1)],
                            wmoe_sb[:, E * k_:E * (k_ + 1)],
                            start=(k_ == 0), stop=(k_ == KD - 1)))(i, k, slot),
                        waits=[("dve", last_cp), ("wdc", wmoe_wdc),
                               sc_slot_rel[slot]], inc="pe")
        sc_add[i] = s.op("vector",
                         (lambda i_, slot_: lambda e: e.tensor_add(
                             sc_tm[:, E * i_:E * (i_ + 1)], p[slot_][:, 0:E],
                             bgate_bc[:]))(i, slot),
                         waits=[("pe", c_mm), ("dve", c_bgc)], inc="dve")
        sc_slot_rel[slot] = ("dve", sc_add[i])

    for i in range(TT + 4):
        if i < TT:
            ln_stage_a(i)
        if 1 <= i < TT + 1:
            ln_stage_b1(i - 1)
        if 2 <= i < TT + 2:
            ln_stage_b2(i - 2)
        if 3 <= i < TT + 3:
            ln_stage_b3(i - 3)
        if 4 <= i:
            ln_stage_c(i - 4)
    xnT_done_dve = s.c("dve")

    sig_done = {}
    for i in range(TT):
        sig_done[i] = s.op("scalar",
                           (lambda i_: lambda e: e.activation(
                               sig_tm[:, E * i_:E * (i_ + 1)],
                               sc_tm[:, E * i_:E * (i_ + 1)], AF.Sigmoid))(i),
                           waits=[("dve", sc_add[i])], inc="act")
    negm2_done = {}
    for i in range(TT):
        c_srt = s.op("vector",
                     (lambda i_: lambda e: e.max(
                         srt_tm[:, E * i_:E * (i_ + 1)], sig_tm[:, E * i_:E * (i_ + 1)]))(i),
                     waits=[("act", sig_done[i])], inc="dve")
        negm2_done[i] = s.op("vector",
                             (lambda i_: lambda e: e.tensor_scalar_mul(
                                 negm2_t[:, i_:i_ + 1],
                                 srt_tm[:, E * i_ + 1:E * i_ + 2], -1.0))(i),
                             waits=[("dve", c_srt)], inc="dve")
    sgn_done = {}
    for i in range(TT):
        sgn_done[i] = s.op("scalar",
                           (lambda i_: lambda e: e.activation(
                               sc_tm[:, E * i_:E * (i_ + 1)],
                               sig_tm[:, E * i_:E * (i_ + 1)], AF.Sign,
                               bias=negm2_t[:, i_:i_ + 1]))(i),
                           waits=[("dve", negm2_done[i])], inc="act")
    c_w = {}
    for i in range(TT):
        c_msk = s.op("vector",
                     (lambda i_: lambda e: e.tensor_scalar(
                         sc_tm[:, E * i_:E * (i_ + 1)],
                         sc_tm[:, E * i_:E * (i_ + 1)], 1.0, 1.0, ALU.add, ALU.min))(i),
                     waits=[("act", sgn_done[i])], inc="dve")
        c_w[i] = s.op("vector",
                      (lambda i_: lambda e: e.tensor_mul(
                          sig_tm[:, E * i_:E * (i_ + 1)],
                          sc_tm[:, E * i_:E * (i_ + 1)],
                          sig_tm[:, E * i_:E * (i_ + 1)]))(i),
                      waits=[("dve", c_msk)], inc="dve")

    ctx_ln.close()
    ctx_at = contextlib.ExitStack()
    xs_t = ctx_at.enter_context(nc.sbuf_tensor("xs_t", [128, T], F32))
    a_t = ctx_at.enter_context(nc.sbuf_tensor("a_t", [128, T], F32))
    tscr_v = [ctx_at.enter_context(nc.sbuf_tensor(f"tscr_v{i}", [128, 512], F32)) for i in range(2)]
    wd_bf = w1buf

    castdma(lambda: wd_bf[:, KD * HALF // 2:].rearrange("p (k e) -> p k e", e=D),
            lambda: AWD.rearrange("(k p) e -> p k e", p=128), sem="wda")
    attnw_wd = s.c("wda")
    nexp_list = list(range(nexp)) + ["sh"]

    def w1_src(eidx, half, k0=0, k1=KD):
        if eidx == "sh":
            w = WS1T
        else:
            w = W1T[eidx]
        return w[128 * k0:128 * k1, HALF * half:HALF * (half + 1)].rearrange(
            "(k p) h -> p k h", p=128)

    def w2_src(eidx, half):
        if eidx == "sh":
            return WS2T[HALF * half:HALF * (half + 1), :].rearrange("(k p) d -> p k d", p=128)
        return W2T[eidx, HALF * half:HALF * (half + 1), :].rearrange("(k p) d -> p k d", p=128)

    w1_wd = {}
    castdma(lambda: w1buf[:, 0:4 * HALF].rearrange("p (k h) -> p k h", h=HALF),
            lambda: w1_src(nexp_list[0], 0, 0, 4), sem="w1d")

    tv_rel = {0: ("dve", 0), 1: ("dve", 0)}
    scan_done = 0
    xs_w_act = {}
    prev_scan = 0
    for m in range(KD):
        banks = (0, 1, 2) if m % 2 == 0 else (3, 4, 5)
        for n in range(NT):
            vsl = n % 2
            projs = [
                ("g", banks[0]),
                ("v", banks[1]),
                ("d", banks[2]),
            ]
            cm = {}
            for pi, (pn, bank) in enumerate(projs):
                c_mm = 0
                for k in range(KD):
                    if pn == "g":
                        lhs_fn = (lambda k_, m_: lambda: w2buf[:, D * k_ + 128 * m_:D * k_ + 128 * (m_ + 1)])(k, m)
                    elif pn == "v":
                        lhs_fn = (lambda k_, m_: lambda: w2buf[:, KD * D + D * k_ + 128 * m_:KD * D + D * k_ + 128 * (m_ + 1)])(k, m)
                    else:
                        lhs_fn = (lambda k_, m_: lambda: wd_bf[:, 8192 + D * k_ + 128 * m_:8192 + D * k_ + 128 * (m_ + 1)])(k, m)
                    c_mm = s.op("tensor",
                                (lambda lf, k_, n_, bank_: lambda e: e.matmul(
                                    p[bank_][:],
                                    lf(),
                                    xnT[:, T * k_ + 512 * n_:T * k_ + 512 * (n_ + 1)],
                                    start=(k_ == 0), stop=(k_ == KD - 1)))(lhs_fn, k, n, bank),
                                waits=[("wda", attnw_wd), ("dve", xnT_done_dve),
                                       bank_rel[bank]], inc="pe")
                cm[pn] = c_mm
            c_sg = s.op("scalar",
                        (lambda m_, n_, b_: lambda e: e.activation(
                            xs_t[:, 512 * n_:512 * (n_ + 1)], p[b_][:], AF.Sigmoid,
                            bias=battn_sb[:, 0 * KD + m_:0 * KD + m_ + 1]))(m, n, banks[0]),
                        waits=[("pe", cm["g"]), ("dve", prev_scan)], inc="act")
            bank_rel[banks[0]] = ("act", c_sg)
            c_tv = s.op("scalar",
                        (lambda m_, n_, b_, vs_: lambda e: e.activation(
                            tscr_v[vs_][:], p[b_][:], AF.Tanh,
                            bias=battn_sb[:, 1 * KD + m_:1 * KD + m_ + 1]))(m, n, banks[1], vsl),
                        waits=[("pe", cm["v"]), tv_rel[vsl]], inc="act")
            bank_rel[banks[1]] = ("act", c_tv)
            c_sd = s.op("scalar",
                        (lambda m_, n_, b_: lambda e: e.activation(
                            a_t[:, 512 * n_:512 * (n_ + 1)], p[b_][:], AF.Sigmoid,
                            bias=battn_sb[:, 2 * KD + m_:2 * KD + m_ + 1]))(m, n, banks[2]),
                        waits=[("pe", cm["d"]), ("dve", prev_scan)], inc="act")
            bank_rel[banks[2]] = ("act", c_sd)
            c_xs = s.op("vector",
                        (lambda n_, vs_: lambda e: e.tensor_mul(
                            xs_t[:, 512 * n_:512 * (n_ + 1)],
                            xs_t[:, 512 * n_:512 * (n_ + 1)], tscr_v[vs_][:]))(n, vsl),
                        waits=[("act", c_sg), ("act", c_tv)], inc="dve")
            tv_rel[vsl] = ("dve", c_xs)
            c_a = s.op("vector",
                       (lambda n_: lambda e: e.tensor_scalar(
                           a_t[:, 512 * n_:512 * (n_ + 1)],
                           a_t[:, 512 * n_:512 * (n_ + 1)], 0.998, 0.001,
                           ALU.mult, ALU.add))(n),
                       waits=[("act", c_sd)], inc="dve")
            wt_i = m * NT + n
            if wt_i < TT:
                tsl = wt_i % 2
                c_wt = s.op("tensor",
                            (lambda i_, tsl_: lambda e: e.transpose(
                                ptF[0:E, PTS[tsl_]:PTS[tsl_] + 128],
                                sig_tm[:, E * i_:E * (i_ + 1)], identf_sb[:]))(wt_i, tsl),
                            waits=[("dve", c_w[wt_i]), ptF_rel[tsl]], inc="pe")
                wT_done = s.op("vector",
                               (lambda i_, tsl_: lambda e: e.tensor_copy(
                                   wT[0:E, 128 * i_:128 * (i_ + 1)],
                                   ptF[0:E, PTS[tsl_]:PTS[tsl_] + 128]))(wt_i, tsl),
                               waits=[("pe", c_wt)], inc="dve")
                ptF_rel[tsl] = ("dve", wT_done)
        prev_scan = s.op("vector",
                         (lambda m_: lambda e: e.tensor_tensor_scan(
                             acc[:, T * m_:T * (m_ + 1)], a_t[:], xs_t[:], 0.0,
                             ALU.mult, ALU.add))(m),
                         waits=[("dve", s.c("dve"))], inc="dve")
        scan_done = prev_scan
    attn_pe_done = s.c("pe")

    ctx_at.close()
    ctx_sc.close()
    ctx_moe = contextlib.ExitStack()
    h_sb = ctx_moe.enter_context(nc.sbuf_tensor("h_sb", [128, NH * 512], BF16))
    tmp_eo = ctx_moe.enter_context(nc.sbuf_tensor("tmp_eo", [128, 512], F32))
    fxbuf = [ctx_moe.enter_context(nc.sbuf_tensor(f"fxbuf{i}", [128, 512], F32)) for i in range(2)]
    out_tm = [ctx_moe.enter_context(nc.sbuf_tensor(f"out_tm{i}", [128, 512], F32)) for i in range(4)]
    fx_rel = {0: ("dve", 0), 1: ("dve", 0)}
    ot_rel = {0: ("ou0", 0), 1: ("ou1", 0), 2: ("ou2", 0), 3: ("ou3", 0)}
    acc_evac_nd = {}

    def fin_slot_ap(q):
        if q < 2:
            return ptF[:, PTS[q]:PTS[q] + 128]
        return p[2 + q][:, 0:128]

    def emit_final_half(i, hh, n_grp):
        if not hasattr(emit_final_half, "tslrel"):
            emit_final_half.tslrel = {0: ptF_rel[0], 1: ptF_rel[1],
                                      2: bank_rel[4], 3: bank_rel[5]}
        if True:
            xsl = (2 * i + hh) % 2
            xd_i = s.op("gpsimd",
                        (lambda i_, hh_, xsl_: lambda e: e.dma_start(
                            out=fxbuf[xsl_][:],
                            in_=X[128 * i_:128 * (i_ + 1), 512 * hh_:512 * (hh_ + 1)]))(i, hh, xsl),
                        waits=[fx_rel[xsl], ("dve", scan_done), ("pe", attn_pe_done)],
                        inc=f"fx{xsl}", inc_k=16)
            osl = (2 * i + hh) % 4
            last_add = 0
            for cc in range(4):
                c = 4 * hh + cc
                q = (i * KD + c) % 4
                rel = emit_final_half.tslrel[q]
                c_tp = s.op("tensor",
                            (lambda i_, c_, q_: lambda e: e.transpose(
                                fin_slot_ap(q_),
                                acc[:, T * c_ + 128 * i_:T * c_ + 128 * (i_ + 1)],
                                identf_sb[:]))(i, c, q),
                            waits=[acc_evac_nd[(n_grp, c)], rel], inc="pe")
                last_add = s.op("vector",
                                (lambda i_, c_, cc_, xsl_, osl_, q_: lambda e: e.tensor_add(
                                    out_tm[osl_][:, 128 * cc_:128 * (cc_ + 1)],
                                    fin_slot_ap(q_),
                                    fxbuf[xsl_][:, 128 * cc_:128 * (cc_ + 1)]))(i, c, cc, xsl, osl, q),
                                waits=[("pe", c_tp), (f"fx{xsl}", xd_i), ot_rel[osl]],
                                inc="dve")
                emit_final_half.tslrel[q] = ("dve", last_add)
            ousem = f"ou{osl}"
            oc = s.op("sync",
                      (lambda i_, hh_, osl_: lambda e: e.dma_start(
                          out=OUT[128 * i_:128 * (i_ + 1), 512 * hh_:512 * (hh_ + 1)],
                          in_=out_tm[osl_][:]))(i, hh, osl),
                      waits=[("dve", last_add)], inc=ousem, inc_k=16)
            ot_rel[osl] = (ousem, oc)
            fx_rel[xsl] = ("dve", last_add)

    w2_wd = {}
    mm1_done = {}
    mm2_done = {}
    gelu_cnt = {}
    acc_last_for_ntile = {n: ("dve", scan_done) for n in range(NT)}
    wb_rel = ("dve", 0)
    wrow_rel = ("pe", 0)
    last_wb_read = 0
    wb_done_act = 0
    last_tmp_read = 0

    n_idx = len(nexp_list) * 2
    for idx in range(n_idx):
        eidx = nexp_list[idx // 2]
        half = idx % 2
        if idx == 0:
            for kk in range(4, KD):
                castdma(
                    (lambda kk_: lambda: w1buf[:, kk_ * HALF:(kk_ + 1) * HALF])(kk),
                    (lambda kk_: lambda: w1_src(nexp_list[0], 0, kk_, kk_ + 1))(kk),
                    waits=[("pe", attn_pe_done)], sem=f"w1b{kk}")
            w1_wd[0] = s.c("w1d")
        else:
            w1_wd[idx] = castdma(
                lambda ei=eidx, h=half: w1buf[:].rearrange("p (k h2) -> p k h2", h2=HALF),
                lambda ei=eidx, h=half: w1_src(ei, h),
                waits=[("pe", mm1_done[idx - 1])], sem="w1d")
        w2_wd[idx] = castdma(
            lambda ei=eidx, h=half: w2buf[:].rearrange("p (k d2) -> p k d2", d2=D),
            lambda ei=eidx, h=half: w2_src(ei, h),
            waits=[("pe", attn_pe_done if idx == 0 else mm2_done[idx - 1])], sem="w2d")

        if half == 0 and eidx != "sh":
            c_wr = s.op("sync",
                        (lambda e_=eidx: lambda e: e.dma_start(
                            out=wrow0[:], in_=wT[e_:e_ + 1, :]))(),
                        waits=[("dve", wT_done), wrow_rel], inc="wr", inc_k=16)
            c_bc = {}
            wb_cp = {}
            for n in range(NT):
                bk = 4 + n % 2
                c_bc[n] = s.op("tensor",
                               (lambda n_, bk_: lambda e: e.matmul(
                                   p[bk_][:], onesb_sb[:],
                                   wrow0[:, 512 * n_:512 * (n_ + 1)],
                                   start=True, stop=True))(n, bk),
                               waits=[("wr", c_wr), bank_rel[bk], wb_rel], inc="pe")
                wb_cp[n] = s.op("scalar",
                                (lambda n_, bk_: lambda e: e.activation(
                                    wb[:, 512 * n_:512 * (n_ + 1)], p[bk_][:],
                                    AF.Identity))(n, bk),
                                waits=[("pe", c_bc[n])], inc="act")
                bank_rel[bk] = ("act", wb_cp[n])
            wb_done_act = s.c("act")
            wrow_rel = ("pe", c_bc[NT - 1])

        mm1_last_pe = 0
        for n in range(NT):
            for mm in range(NH):
                bank = mm % 2
                c_mm = 0
                for k in range(KD):
                    if idx == 0:
                        wsem = (f"w1b{k}", 16) if k >= 4 else ("w1d", w1_wd[0])
                        xwaits = [wsem, bank_rel[bank], ("dve", scan_done)]
                    else:
                        xwaits = [("w1d", w1_wd[idx]), bank_rel[bank]]
                    c_mm = s.op("tensor",
                                (lambda k_, mm_, n_, b_: lambda e: e.matmul(
                                    p[b_][:],
                                    w1buf[:, HALF * k_ + 128 * mm_:HALF * k_ + 128 * (mm_ + 1)],
                                    xnT[:, T * k_ + 512 * n_:T * k_ + 512 * (n_ + 1)],
                                    start=(k_ == 0), stop=(k_ == KD - 1)))(k, mm, n, bank),
                                waits=xwaits, inc="pe")
                mm1_last_pe = c_mm
                bcol = (E * 32 + 16 * half + mm) if eidx == "sh" else (32 * eidx + 16 * half + mm)
                gwaits = [("pe", c_mm)]
                if idx == 0:
                    gwaits.append(("dve", scan_done))
                gelu_cnt[(idx, n, mm)] = s.op(
                    "scalar",
                    (lambda mm_, b_, bc_: lambda e: e.activation(
                        h_sb[:, 512 * mm_:512 * (mm_ + 1)], p[b_][:], AF.Gelu,
                        bias=b1all[:, bc_:bc_ + 1]))(mm, bank, bcol),
                    waits=gwaits, inc="act")
                bank_rel[bank] = ("act", gelu_cnt[(idx, n, mm)])
            for d in range(ND):
                bank = 2 + d % 2
                c_mm = 0
                sh_tail = (eidx == "sh" and half == 1)
                for k in range(NH):
                    is_stop = (k == NH - 1) and not sh_tail
                    c_mm = s.op("tensor",
                                (lambda k_, d_, n_, b_, st_: lambda e: e.matmul(
                                    p[b_][:],
                                    w2buf[:, D * k_ + 128 * d_:D * k_ + 128 * (d_ + 1)],
                                    h_sb[:, 512 * k_:512 * (k_ + 1)],
                                    start=(k_ == 0), stop=st_))(k, d, n, bank, is_stop),
                                waits=[("w2d", w2_wd[idx]),
                                       ("act", gelu_cnt[(idx, n, k)]),
                                       bank_rel[bank]], inc="pe")
                if sh_tail:
                    c_mm = s.op("tensor",
                                (lambda d_, n_, b_: lambda e: e.matmul(
                                    p[b_][:],
                                    b2sb[:, 128 * d_:128 * (d_ + 1)],
                                    wT[:, 512 * n_:512 * (n_ + 1)],
                                    start=False, stop=True))(d, n, bank),
                                waits=[("dve", wT_done)], inc="pe")
                mm2_done_grp = c_mm
                prev = acc_last_for_ntile[n]
                if eidx == "sh":
                    if half == 0:
                        c_ev = s.op("vector",
                                    (lambda d_, n_, b_: lambda e: e.tensor_add(
                                        acc[:, T * d_ + 512 * n_:T * d_ + 512 * (n_ + 1)],
                                        acc[:, T * d_ + 512 * n_:T * d_ + 512 * (n_ + 1)],
                                        p[b_][:]))(d, n, bank),
                                    waits=[("pe", mm2_done_grp), prev], inc="dve")
                    else:
                        c_ev = s.op("vector",
                                    (lambda d_, n_, b_: lambda e: e.scalar_tensor_tensor(
                                        acc[:, T * d_ + 512 * n_:T * d_ + 512 * (n_ + 1)],
                                        p[b_][:], bs2_sb[:, d_:d_ + 1],
                                        acc[:, T * d_ + 512 * n_:T * d_ + 512 * (n_ + 1)],
                                        ALU.add, ALU.add))(d, n, bank),
                                    waits=[("pe", mm2_done_grp), prev], inc="dve")
                    bank_rel[bank] = ("dve", c_ev)
                else:
                    c_t = s.op("vector",
                               (lambda d_, n_, b_: lambda e: e.tensor_mul(
                                   tmp_eo[:], p[b_][:],
                                   wb[:, 512 * n_:512 * (n_ + 1)]))(d, n, bank),
                               waits=[("pe", mm2_done_grp), ("act", wb_done_act),
                                      ("dve", last_tmp_read)],
                               inc="dve")
                    bank_rel[bank] = ("dve", c_t)
                    last_wb_read = c_t
                    c_ev = s.op("vector",
                                (lambda d_, n_: lambda e: e.tensor_add(
                                    acc[:, T * d_ + 512 * n_:T * d_ + 512 * (n_ + 1)],
                                    acc[:, T * d_ + 512 * n_:T * d_ + 512 * (n_ + 1)],
                                    tmp_eo[:]))(d, n),
                                waits=[prev, ("dve", c_t)], inc="dve")
                    last_tmp_read = c_ev
                acc_last_for_ntile[n] = ("dve", c_ev)
                acc_evac_nd[(n, d)] = ("dve", c_ev)
                if idx == n_idx - 1 and n >= 1:
                    emit_final_half(4 * (n - 1) + d // 2, d % 2, n - 1)
                if idx == n_idx - 1 and n == NT - 1 and d == 3:
                    for ti in range(4):
                        emit_final_half(4 * n + ti, 0, n)
            if idx == n_idx - 1 and n == NT - 1:
                for ti in range(4):
                    emit_final_half(4 * n + ti, 1, n)
        mm1_done[idx] = mm1_last_pe
        mm2_done[idx] = s.c("pe")
        if half == 1 and eidx != "sh":
            wb_rel = ("dve", last_wb_read)

    s.op("sync", lambda e: e.nop(),
         waits=[("ou0", s.c("ou0")), ("ou1", s.c("ou1")),
                ("ou2", s.c("ou2")), ("ou3", s.c("ou3"))])
    if debug:
        dbg_list = [("d_wT", wT, BF16, [E, T]),
                    ("d_wb", wb, BF16, [128, T]),
                    ("d_bgc", bgate_bc, F32, [128, E]),
                    ("d_xnT", xnT, BF16, [128, KD * T]),
                    ("d_acc", acc, F32, [128, KD * T])]
        dl = 0
        for nm, t_, dty, shp in dbg_list:
            od = nc.dram_tensor(nm, shp, dty, kind="ExternalOutput").ap()
            dl = s.op("sync",
                      (lambda od_, t2: lambda e: e.dma_start(out=od_[:, :], in_=t2[:]))(od, t_),
                      waits=[("ou0", s.c("ou0")), ("ou1", s.c("ou1")),
                             ("ou2", s.c("ou2")), ("ou3", s.c("ou3"))],
                      inc="wr", inc_k=16)
        s.op("sync", lambda e: e.nop(), waits=[("wr", dl)])

    with nc.Block() as block:
        s.emit(block, sems)

    n_instr = sum(len(v) for v in s.ops.values())
    ctx_moe.close()
    ctx.close()
    return nc, n_instr


def host_inputs(inputs):
    f = np.ascontiguousarray
    base = {
        "awg": f(inputs["L0_attn_Wg"]),
        "awv": f(inputs["L0_attn_Wv"]),
        "awd": f(inputs["L0_attn_Wd"]),
        "battn": f(np.stack([
            inputs["L0_attn_bg"].reshape(KD, 128).T,
            inputs["L0_attn_bv"].reshape(KD, 128).T,
            inputs["L0_attn_bd"].reshape(KD, 128).T])),
        "wmoet": f(inputs["L0_ffn_Wg"].T),
        "bgr1": f(inputs["L0_ffn_bg"].reshape(1, E)),
        "bgr2": f(inputs["L0_ffn_bias"].reshape(1, E)),
        "w1t": f(inputs["L0_ffn_W1"].transpose(0, 2, 1)),
        "b1p": f(inputs["L0_ffn_b1"].reshape(E, 32, 128).transpose(0, 2, 1)),
        "w2t": f(inputs["L0_ffn_W2"].transpose(0, 2, 1)),
        "b2n": f(inputs["L0_ffn_b2"]),
        "ws1t": f(inputs["L0_ffn_Ws1"].T),
        "bs1p": f(inputs["L0_ffn_bs1"].reshape(32, 128).T),
        "ws2t": f(inputs["L0_ffn_Ws2"].T),
        "bs2p": f(inputs["L0_ffn_bs2"].reshape(KD, 128).T),
        "identf": np.eye(128, dtype=np.float32),
        "onesb": np.ones((1, 128), dtype=ml_dtypes.bfloat16),
        "onesf": np.ones((1, 128), dtype=np.float32),
    }
    x = np.asarray(inputs["L0_attn_x"], dtype=np.float32)
    maps = []
    for c in range(8):
        m = dict(base)
        m["x"] = f(x[c])
        maps.append(m)
    return maps


def kernel(**inputs):
    nc, _ = build_program(nexp=8)
    in_maps = host_inputs(inputs)
    res = run_bass_kernel_spmd(nc, in_maps, core_ids=list(range(8)))
    return np.stack([r["out"] for r in res.results], axis=0)


if __name__ == "__main__":
    import sys
    nexp = int(sys.argv[1]) if len(sys.argv) > 1 else 1
    rng = np.random.default_rng(0)
    sc = 0.02
    inputs = {
        "L0_attn_x": rng.standard_normal((8, T, D)).astype(np.float32),
        "L0_attn_Wg": (rng.standard_normal((D, D)) / np.sqrt(D)).astype(np.float32),
        "L0_attn_bg": (rng.standard_normal(D) * 0.1).astype(np.float32),
        "L0_attn_Wv": (rng.standard_normal((D, D)) / np.sqrt(D)).astype(np.float32),
        "L0_attn_bv": (rng.standard_normal(D) * 0.1).astype(np.float32),
        "L0_attn_Wd": (rng.standard_normal((D, D)) / np.sqrt(D)).astype(np.float32),
        "L0_attn_bd": (rng.standard_normal(D) * 0.5 + 1).astype(np.float32),
        "L0_ffn_Wg": (rng.standard_normal((E, D)) * sc).astype(np.float32),
        "L0_ffn_bg": (rng.standard_normal(E) * 0.1).astype(np.float32),
        "L0_ffn_bias": (rng.standard_normal(E) * 0.1).astype(np.float32),
        "L0_ffn_W1": (rng.standard_normal((E, H, D)) * sc).astype(np.float32),
        "L0_ffn_b1": (rng.standard_normal((E, H)) * 0.05).astype(np.float32),
        "L0_ffn_W2": (rng.standard_normal((E, D, H)) * sc).astype(np.float32),
        "L0_ffn_b2": (rng.standard_normal((E, D)) * 0.05).astype(np.float32),
        "L0_ffn_Ws1": (rng.standard_normal((H, D)) * sc).astype(np.float32),
        "L0_ffn_bs1": (rng.standard_normal(H) * 0.05).astype(np.float32),
        "L0_ffn_Ws2": (rng.standard_normal((D, H)) * sc).astype(np.float32),
        "L0_ffn_bs2": (rng.standard_normal(D) * 0.05).astype(np.float32),
    }

    def np_ref(c, nexp):
        from scipy.special import erf
        x = inputs["L0_attn_x"][c].astype(np.float64)
        m = x.mean(1, keepdims=True); v = x.var(1, keepdims=True)
        xn = (x - m) / np.sqrt(v + 1e-5)
        g = xn @ inputs["L0_attn_Wg"] + inputs["L0_attn_bg"]
        vv = xn @ inputs["L0_attn_Wv"] + inputs["L0_attn_bv"]
        d = xn @ inputs["L0_attn_Wd"] + inputs["L0_attn_bd"]
        sig = lambda z: 1 / (1 + np.exp(-z))
        xs = sig(g) * np.tanh(vv)
        a = 0.001 + 0.998 * sig(d)
        h = np.zeros_like(xs); st = np.zeros(D)
        for t in range(T):
            st = a[t] * st + xs[t]
            h[t] = st
        gelu = lambda z: 0.5 * z * (1 + erf(z / np.sqrt(2)))
        scores = sig(xn @ inputs["L0_ffn_Wg"].T + inputs["L0_ffn_bg"] + inputs["L0_ffn_bias"])
        idx = np.argsort(-scores, 1, kind="stable")[:, :2]
        w = np.zeros_like(scores)
        np.put_along_axis(w, idx, np.take_along_axis(scores, idx, 1), 1)
        comb = np.zeros_like(xn)
        for e in range(nexp):
            he = gelu(xn @ inputs["L0_ffn_W1"][e].T + inputs["L0_ffn_b1"][e])
            eo = he @ inputs["L0_ffn_W2"][e].T
            comb += w[:, e:e + 1] * eo
        comb += w[:, :] @ inputs["L0_ffn_b2"]
        shared = gelu(xn @ inputs["L0_ffn_Ws1"].T + inputs["L0_ffn_bs1"]) @ \
            inputs["L0_ffn_Ws2"].T + inputs["L0_ffn_bs2"]
        return x + h + shared + comb

    nc, n_instr = build_program(nexp=nexp)
    print("instructions:", n_instr)
    in_maps = host_inputs(inputs)
    res = run_bass_kernel_spmd(nc, in_maps[:1], core_ids=[0])
    got = res.results[0]["out"]
    ref = np_ref(0, nexp)
    err = np.linalg.norm(got - ref) / np.linalg.norm(ref)
    print("rel l2:", err)
    rvr = float(np.mean((got - ref) ** 2) / np.mean(ref ** 2))
    print("rvr:", rvr)


# version 16
# speedup vs baseline: 1.0039x; 1.0039x over previous
import contextlib

import numpy as np
import ml_dtypes

import concourse.bass as bass
import concourse.mybir as mybir
from concourse.bass_utils import run_bass_kernel_spmd

F32 = mybir.dt.float32
BF16 = mybir.dt.bfloat16
AF = mybir.ActivationFunctionType
ALU = mybir.AluOpType
AX = mybir.AxisListType

D = 1024
H = 4096
E = 8
T = 2048
KD = D // 128
TT = T // 128
NT = T // 512
NH = 16
ND = 8
HALF = 2048


class Sched:

    ENGINES = ("tensor", "scalar", "vector", "gpsimd", "sync")

    def __init__(self):
        self.ops = {e: [] for e in self.ENGINES}
        self.cnt = {}
        self.last_wait = {e: {} for e in self.ENGINES}

    def op(self, eng, fn, waits=(), inc=None, inc_k=1):
        w = []
        for sem, val in waits:
            if val <= 0:
                continue
            if self.last_wait[eng].get(sem, -1) >= val:
                continue
            self.last_wait[eng][sem] = val
            w.append((sem, val))
        if inc is not None:
            self.cnt[inc] = self.cnt.get(inc, 0) + inc_k
        self.ops[eng].append((fn, w, inc, inc_k))
        return self.cnt.get(inc, 0)

    def c(self, sem):
        return self.cnt.get(sem, 0)

    def emit(self, block, sems):
        def mk(eng_name):
            def body(e):
                for fn, waits, inc, inc_k in self.ops[eng_name]:
                    for sem, val in waits:
                        e.wait_ge(sems[sem], val)
                    instr = fn(e)
                    if inc is not None:
                        instr.then_inc(sems[inc], inc_k)
            return body
        for eng_name in self.ENGINES:
            if self.ops[eng_name]:
                getattr(block, eng_name)(mk(eng_name))


def build_program(nexp=8, debug=False):
    nc = bass.Bass("TRN2", target_bir_lowering=False)

    dt = lambda n, s: nc.dram_tensor(n, s, F32, kind="ExternalInput").ap()
    X = dt("x", [T, D])
    AWG = dt("awg", [D, D]); AWV = dt("awv", [D, D]); AWD = dt("awd", [D, D])
    BATTN = dt("battn", [3, 128, KD])
    WMOET = dt("wmoet", [D, E])
    BGR1 = dt("bgr1", [1, E]); BGR2 = dt("bgr2", [1, E])
    W1T = dt("w1t", [E, D, H])
    B1P = dt("b1p", [E, 128, 32])
    W2T = dt("w2t", [E, H, D])
    B2N = dt("b2n", [E, D])
    WS1T = dt("ws1t", [D, H])
    BS1P = dt("bs1p", [128, 32])
    WS2T = dt("ws2t", [H, D])
    BS2P = dt("bs2p", [128, KD])
    IDENTF = nc.dram_tensor("identf", [128, 128], F32, kind="ExternalInput").ap()
    ONESB = nc.dram_tensor("onesb", [1, 128], BF16, kind="ExternalInput").ap()
    ONESF = nc.dram_tensor("onesf", [1, 128], F32, kind="ExternalInput").ap()
    OUT = nc.dram_tensor("out", [T, D], F32, kind="ExternalOutput").ap()

    s = Sched()
    ctx = contextlib.ExitStack()
    sb = lambda n, shape, dty: ctx.enter_context(nc.sbuf_tensor(n, shape, dty))

    xnT = sb("xnT", [128, KD * T], BF16)
    acc = sb("acc", [128, KD * T], F32)
    w1buf = sb("w1buf", [128, KD * HALF], BF16)
    w2buf = sb("w2buf", [128, NH * D], BF16)
    wT = sb("wT", [E, T], BF16)
    b2sb = sb("b2sb", [E, D], BF16)
    wmoe_sb = sb("wmoe_sb", [128, KD * E], BF16)
    b1all = sb("b1all", [128, E * 32 + 32], F32)
    battn_sb = sb("battn_sb", [128, 3 * KD], F32)
    bs2_sb = sb("bs2_sb", [128, KD], F32)
    bgr1_sb = sb("bgr1_sb", [1, E], F32)
    bgr2_sb = sb("bgr2_sb", [1, E], F32)
    bgrow = sb("bgrow", [1, E], F32)
    bgate_bc = sb("bgate_bc", [128, E], F32)
    identf_sb = sb("identf_sb", [128, 128], F32)
    onesb_sb = sb("onesb_sb", [1, 128], BF16)
    onesf_sb = sb("onesf_sb", [1, 128], F32)
    eps_sb = sb("eps_sb", [128, 1], F32)
    stats = sb("stats", [128, 6 * TT], F32)
    wrow0 = sb("wrow0", [1, T], BF16)
    wb = sb("wb", [128, T], BF16)

    p = [ctx.enter_context(nc.psum_tensor(f"p{i}", [128, 512], F32)) for i in range(6)]
    ptF = ctx.enter_context(nc.psum_tensor("ptF", [128, 1024], F32))
    PTS = (0, 512)

    sem_names = ["cst", "xl0", "xl1", "xl2", "xl3", "wr",
                 "fx0", "fx1", "ou0", "ou1", "ou2", "ou3",
                 "wdc", "wda", "w1d", "w2d", "pe", "act", "dve",
                 "w1b4", "w1b5", "w1b6", "w1b7"]
    sems = {n: ctx.enter_context(nc.semaphore(n)) for n in sem_names}

    bank_rel = {i: ("pe", 0) for i in range(6)}
    ptF_rel = {0: ("pe", 0), 1: ("pe", 0)}

    def syncdma(dst_ap_fn, src_ap_fn, sem="cst", waits=(), eng="scalar"):
        return s.op(eng, lambda e: e.dma_start(out=dst_ap_fn(), in_=src_ap_fn()),
                    waits=waits, inc=sem, inc_k=16)

    syncdma(lambda: identf_sb[:], lambda: IDENTF[:, :])
    syncdma(lambda: onesb_sb[:], lambda: ONESB[:, :])
    syncdma(lambda: onesf_sb[:], lambda: ONESF[:, :])
    syncdma(lambda: battn_sb[:].rearrange("p (j c) -> p j c", c=KD),
            lambda: BATTN.rearrange("j p c -> p j c"))
    syncdma(lambda: b1all[:, 0:E * 32].rearrange("p (e c) -> p e c", c=32),
            lambda: B1P.rearrange("e p c -> p e c"))
    syncdma(lambda: b1all[:, E * 32:E * 32 + 32], lambda: BS1P[:, :])
    syncdma(lambda: bs2_sb[:], lambda: BS2P[:, :])
    syncdma(lambda: bgr1_sb[:], lambda: BGR1[:, :])
    syncdma(lambda: bgr2_sb[:], lambda: BGR2[:, :])
    consts_xd = s.c("cst")

    def castdma(dst_ap_fn, src_ap_fn, waits=(), sem="wdc"):
        return s.op("gpsimd", lambda e: e.dma_start(out=dst_ap_fn(), in_=src_ap_fn()),
                    waits=waits, inc=sem, inc_k=16)

    castdma(lambda: b2sb[:], lambda: B2N[:, :])
    castdma(lambda: wmoe_sb[:].rearrange("p (k e) -> p k e", e=E),
            lambda: WMOET.rearrange("(k p) e -> p k e", p=128))
    wmoe_wdc = s.c("wdc")
    castdma(lambda: w2buf[:, 0:KD * D].rearrange("p (k e) -> p k e", e=D),
            lambda: AWG.rearrange("(k p) e -> p k e", p=128), sem="wda")
    castdma(lambda: w2buf[:, KD * D:2 * KD * D].rearrange("p (k e) -> p k e", e=D),
            lambda: AWV.rearrange("(k p) e -> p k e", p=128), sem="wda")

    ctx_sc = contextlib.ExitStack()
    sc_tm = ctx_sc.enter_context(nc.sbuf_tensor("sc_tm", [128, E * TT], F32))
    sig_tm = ctx_sc.enter_context(nc.sbuf_tensor("sig_tm", [128, E * TT], F32))
    srt_tm = ctx_sc.enter_context(nc.sbuf_tensor("srt_tm", [128, E * TT], F32))
    negm2_t = ctx_sc.enter_context(nc.sbuf_tensor("negm2_t", [128, TT], F32))
    ctx_ln = contextlib.ExitStack()
    xbuf = [ctx_ln.enter_context(nc.sbuf_tensor(f"xbuf{i}", [128, D], F32)) for i in range(4)]
    xnb = [ctx_ln.enter_context(nc.sbuf_tensor(f"xnb{i}", [128, D], F32)) for i in range(2)]
    g_t = ctx_ln.enter_context(nc.sbuf_tensor("g_t", [128, D], F32))

    s.op("vector", lambda e: e.memset(eps_sb[:], 1e-5), inc="dve")

    c_bg = s.op("vector", lambda e: e.tensor_add(bgrow[:], bgr1_sb[:], bgr2_sb[:]),
                waits=[("cst", consts_xd)], inc="dve")
    c_bgmm = s.op("tensor",
                  lambda e: e.matmul(p[4][:, 0:E], onesf_sb[:], bgrow[:],
                                     start=True, stop=True),
                  waits=[("dve", c_bg), bank_rel[4]], inc="pe")
    c_bgc = s.op("vector", lambda e: e.tensor_copy(bgate_bc[:], p[4][:, 0:E]),
                 waits=[("pe", c_bgmm)], inc="dve")
    bank_rel[4] = ("dve", c_bgc)

    xl_cnt = {}
    c_red = {}; c_sqr = {}; c_vn = {}; c_ex = {}; c_b2 = {}; c_id = {}
    cp_done = {}; sc_add = {}
    sc_slot_rel = {i: ("dve", 0) for i in range(4)}
    xrel = {}
    xnb_rel = {0: ("pe", 0), 1: ("pe", 0)}

    def st(col, i):
        return stats[:, col * TT + i:col * TT + i + 1]

    def ln_stage_a(i):
        sl4 = i % 4
        w = [xrel[sl4]] if sl4 in xrel else []
        xl_cnt[i] = s.op("sync",
                         (lambda i_, sl_: lambda e: e.dma_start(
                             out=xbuf[sl_][:], in_=X[128 * i_:128 * (i_ + 1), :]))(i, sl4),
                         waits=w, inc=f"xl{sl4}", inc_k=16)
        c_red[i] = s.op("vector",
                        (lambda i_, sl_: lambda e: e.reduce_sum(
                            st(0, i_), xbuf[sl_][:], axis=AX.X))(i, sl4),
                        waits=[(f"xl{i % 4}", xl_cnt[i])], inc="dve")
        c_sqr[i] = s.op("scalar",
                        (lambda i_, sl_: lambda e: e.activation(
                            g_t[:], xbuf[sl_][:], AF.Square,
                            accum_out=st(2, i_)))(i, sl4),
                        waits=[(f"xl{i % 4}", xl_cnt[i]),
                               ("act", c_sqr.get(i - 1, 0))], inc="act")

    def ln_stage_b1(i):
        c_m = s.op("vector",
                   (lambda i_: lambda e: e.tensor_mul(st(1, i_), st(0, i_), st(0, i_)))(i),
                   waits=[("dve", c_red[i])], inc="dve")
        c_vn[i] = s.op("vector",
                       (lambda i_: lambda e: e.scalar_tensor_tensor(
                           st(3, i_), st(1, i_), -1.0 / D, st(2, i_),
                           ALU.mult, ALU.add))(i),
                       waits=[("dve", c_m), ("act", c_sqr[i])], inc="dve")

    def ln_stage_b2(i):
        c_l = s.op("scalar",
                   (lambda i_: lambda e: e.activation(
                       st(4, i_), st(3, i_), AF.Ln,
                       bias=eps_sb[:, 0:1], scale=1.0 / D))(i),
                   waits=[("dve", c_vn[i])], inc="act")
        c_ex[i] = s.op("scalar",
                       (lambda i_: lambda e: e.activation(
                           st(4, i_), st(4, i_), AF.Exp, scale=-0.5))(i),
                       waits=[("act", c_l)], inc="act")

    def ln_stage_b3(i):
        sl4 = i % 4
        sl2 = i % 2
        c_b2[i] = s.op("vector",
                       (lambda i_: lambda e: e.scalar_tensor_tensor(
                           st(5, i_), st(0, i_), -1.0 / D, st(4, i_),
                           ALU.mult, ALU.mult))(i),
                       waits=[("act", c_ex[i])], inc="dve")
        c_id[i] = s.op("scalar",
                       (lambda i_, sl_, sl2_: lambda e: e.activation(
                           xnb[sl2_][:], xbuf[sl_][:], AF.Identity,
                           bias=st(5, i_), scale=st(4, i_)))(i, sl4, sl2),
                       waits=[("dve", c_b2[i]), xnb_rel[sl2]], inc="act")
        xrel[sl4] = ("act", c_id[i])

    def ln_stage_c(i):
        sl2 = i % 2
        last_tp = 0
        last_cp = 0
        for c in range(KD):
            tsl = (i * KD + c) % 2
            last_tp = s.op("tensor",
                           (lambda sl2_, c_, tsl_: lambda e: e.transpose(
                               ptF[:, PTS[tsl_]:PTS[tsl_] + 128],
                               xnb[sl2_][:, 128 * c_:128 * (c_ + 1)],
                               identf_sb[:]))(sl2, c, tsl),
                           waits=[("act", c_id[i]), ptF_rel[tsl], ("cst", consts_xd)],
                           inc="pe")
            last_cp = s.op("vector",
                           (lambda i_, c_, tsl_: lambda e: e.tensor_copy(
                               xnT[:, T * c_ + 128 * i_:T * c_ + 128 * (i_ + 1)],
                               ptF[:, PTS[tsl_]:PTS[tsl_] + 128]))(i, c, tsl),
                           waits=[("pe", last_tp)], inc="dve")
            ptF_rel[tsl] = ("dve", last_cp)
        xnb_rel[sl2] = ("pe", last_tp)
        cp_done[i] = last_cp
        slot = i % 4
        c_mm = 0
        for k in range(KD):
            c_mm = s.op("tensor",
                        (lambda i_, k_, slot_: lambda e: e.matmul(
                            p[slot_][:, 0:E],
                            xnT[:, T * k_ + 128 * i_:T * k_ + 128 * (i_ + 1)],
                            wmoe_sb[:, E * k_:E * (k_ + 1)],
                            start=(k_ == 0), stop=(k_ == KD - 1)))(i, k, slot),
                        waits=[("dve", last_cp), ("wdc", wmoe_wdc),
                               sc_slot_rel[slot]], inc="pe")
        sc_add[i] = s.op("vector",
                         (lambda i_, slot_: lambda e: e.tensor_add(
                             sc_tm[:, E * i_:E * (i_ + 1)], p[slot_][:, 0:E],
                             bgate_bc[:]))(i, slot),
                         waits=[("pe", c_mm), ("dve", c_bgc)], inc="dve")
        sc_slot_rel[slot] = ("dve", sc_add[i])

    for i in range(TT + 4):
        if i < TT:
            ln_stage_a(i)
        if 1 <= i < TT + 1:
            ln_stage_b1(i - 1)
        if 2 <= i < TT + 2:
            ln_stage_b2(i - 2)
        if 3 <= i < TT + 3:
            ln_stage_b3(i - 3)
        if 4 <= i:
            ln_stage_c(i - 4)
    xnT_done_dve = s.c("dve")

    sig_done = {}
    for i in range(TT):
        sig_done[i] = s.op("scalar",
                           (lambda i_: lambda e: e.activation(
                               sig_tm[:, E * i_:E * (i_ + 1)],
                               sc_tm[:, E * i_:E * (i_ + 1)], AF.Sigmoid))(i),
                           waits=[("dve", sc_add[i])], inc="act")
    negm2_done = {}
    for i in range(TT):
        c_srt = s.op("vector",
                     (lambda i_: lambda e: e.max(
                         srt_tm[:, E * i_:E * (i_ + 1)], sig_tm[:, E * i_:E * (i_ + 1)]))(i),
                     waits=[("act", sig_done[i])], inc="dve")
        negm2_done[i] = s.op("vector",
                             (lambda i_: lambda e: e.tensor_scalar_mul(
                                 negm2_t[:, i_:i_ + 1],
                                 srt_tm[:, E * i_ + 1:E * i_ + 2], -1.0))(i),
                             waits=[("dve", c_srt)], inc="dve")
    sgn_done = {}
    for i in range(TT):
        sgn_done[i] = s.op("scalar",
                           (lambda i_: lambda e: e.activation(
                               sc_tm[:, E * i_:E * (i_ + 1)],
                               sig_tm[:, E * i_:E * (i_ + 1)], AF.Sign,
                               bias=negm2_t[:, i_:i_ + 1]))(i),
                           waits=[("dve", negm2_done[i])], inc="act")
    c_w = {}
    for i in range(TT):
        c_msk = s.op("vector",
                     (lambda i_: lambda e: e.tensor_scalar(
                         sc_tm[:, E * i_:E * (i_ + 1)],
                         sc_tm[:, E * i_:E * (i_ + 1)], 1.0, 1.0, ALU.add, ALU.min))(i),
                     waits=[("act", sgn_done[i])], inc="dve")
        c_w[i] = s.op("vector",
                      (lambda i_: lambda e: e.tensor_mul(
                          sig_tm[:, E * i_:E * (i_ + 1)],
                          sc_tm[:, E * i_:E * (i_ + 1)],
                          sig_tm[:, E * i_:E * (i_ + 1)]))(i),
                      waits=[("dve", c_msk)], inc="dve")

    ctx_ln.close()
    ctx_at = contextlib.ExitStack()
    xs_t = ctx_at.enter_context(nc.sbuf_tensor("xs_t", [128, T], F32))
    a_t = ctx_at.enter_context(nc.sbuf_tensor("a_t", [128, T], F32))
    tscr_v = [ctx_at.enter_context(nc.sbuf_tensor(f"tscr_v{i}", [128, 512], F32)) for i in range(2)]
    wd_bf = w1buf

    castdma(lambda: wd_bf[:, KD * HALF // 2:].rearrange("p (k e) -> p k e", e=D),
            lambda: AWD.rearrange("(k p) e -> p k e", p=128), sem="wda")
    attnw_wd = s.c("wda")
    nexp_list = list(range(nexp)) + ["sh"]

    def w1_src(eidx, half, k0=0, k1=KD):
        if eidx == "sh":
            w = WS1T
        else:
            w = W1T[eidx]
        return w[128 * k0:128 * k1, HALF * half:HALF * (half + 1)].rearrange(
            "(k p) h -> p k h", p=128)

    def w2_src(eidx, half):
        if eidx == "sh":
            return WS2T[HALF * half:HALF * (half + 1), :].rearrange("(k p) d -> p k d", p=128)
        return W2T[eidx, HALF * half:HALF * (half + 1), :].rearrange("(k p) d -> p k d", p=128)

    w1_wd = {}
    castdma(lambda: w1buf[:, 0:4 * HALF].rearrange("p (k h) -> p k h", h=HALF),
            lambda: w1_src(nexp_list[0], 0, 0, 4), sem="w1d")

    tv_rel = {0: ("dve", 0), 1: ("dve", 0)}
    scan_done = 0
    xs_w_act = {}
    prev_scan = 0
    for m in range(KD):
        banks = (0, 1, 2) if m % 2 == 0 else (3, 4, 5)
        for n in range(NT):
            vsl = n % 2
            projs = [
                ("g", banks[0]),
                ("v", banks[1]),
                ("d", banks[2]),
            ]
            cm = {}
            for pi, (pn, bank) in enumerate(projs):
                c_mm = 0
                for k in range(KD):
                    if pn == "g":
                        lhs_fn = (lambda k_, m_: lambda: w2buf[:, D * k_ + 128 * m_:D * k_ + 128 * (m_ + 1)])(k, m)
                    elif pn == "v":
                        lhs_fn = (lambda k_, m_: lambda: w2buf[:, KD * D + D * k_ + 128 * m_:KD * D + D * k_ + 128 * (m_ + 1)])(k, m)
                    else:
                        lhs_fn = (lambda k_, m_: lambda: wd_bf[:, 8192 + D * k_ + 128 * m_:8192 + D * k_ + 128 * (m_ + 1)])(k, m)
                    c_mm = s.op("tensor",
                                (lambda lf, k_, n_, bank_: lambda e: e.matmul(
                                    p[bank_][:],
                                    lf(),
                                    xnT[:, T * k_ + 512 * n_:T * k_ + 512 * (n_ + 1)],
                                    start=(k_ == 0), stop=(k_ == KD - 1)))(lhs_fn, k, n, bank),
                                waits=[("wda", attnw_wd), ("dve", xnT_done_dve),
                                       bank_rel[bank]], inc="pe")
                cm[pn] = c_mm
            c_sg = s.op("scalar",
                        (lambda m_, n_, b_: lambda e: e.activation(
                            xs_t[:, 512 * n_:512 * (n_ + 1)], p[b_][:], AF.Sigmoid,
                            bias=battn_sb[:, 0 * KD + m_:0 * KD + m_ + 1]))(m, n, banks[0]),
                        waits=[("pe", cm["g"]), ("dve", prev_scan)], inc="act")
            bank_rel[banks[0]] = ("act", c_sg)
            c_tv = s.op("scalar",
                        (lambda m_, n_, b_, vs_: lambda e: e.activation(
                            tscr_v[vs_][:], p[b_][:], AF.Tanh,
                            bias=battn_sb[:, 1 * KD + m_:1 * KD + m_ + 1]))(m, n, banks[1], vsl),
                        waits=[("pe", cm["v"]), tv_rel[vsl]], inc="act")
            bank_rel[banks[1]] = ("act", c_tv)
            c_sd = s.op("scalar",
                        (lambda m_, n_, b_: lambda e: e.activation(
                            a_t[:, 512 * n_:512 * (n_ + 1)], p[b_][:], AF.Sigmoid,
                            bias=battn_sb[:, 2 * KD + m_:2 * KD + m_ + 1]))(m, n, banks[2]),
                        waits=[("pe", cm["d"]), ("dve", prev_scan)], inc="act")
            bank_rel[banks[2]] = ("act", c_sd)
            c_xs = s.op("vector",
                        (lambda n_, vs_: lambda e: e.tensor_mul(
                            xs_t[:, 512 * n_:512 * (n_ + 1)],
                            xs_t[:, 512 * n_:512 * (n_ + 1)], tscr_v[vs_][:]))(n, vsl),
                        waits=[("act", c_sg), ("act", c_tv)], inc="dve")
            tv_rel[vsl] = ("dve", c_xs)
            c_a = s.op("vector",
                       (lambda n_: lambda e: e.tensor_scalar(
                           a_t[:, 512 * n_:512 * (n_ + 1)],
                           a_t[:, 512 * n_:512 * (n_ + 1)], 0.998, 0.001,
                           ALU.mult, ALU.add))(n),
                       waits=[("act", c_sd)], inc="dve")
            wt_i = m * NT + n
            if wt_i < TT:
                tsl = wt_i % 2
                c_wt = s.op("tensor",
                            (lambda i_, tsl_: lambda e: e.transpose(
                                ptF[0:E, PTS[tsl_]:PTS[tsl_] + 128],
                                sig_tm[:, E * i_:E * (i_ + 1)], identf_sb[:]))(wt_i, tsl),
                            waits=[("dve", c_w[wt_i]), ptF_rel[tsl]], inc="pe")
                wT_done = s.op("scalar",
                               (lambda i_, tsl_: lambda e: e.activation(
                                   wT[0:E, 128 * i_:128 * (i_ + 1)],
                                   ptF[0:E, PTS[tsl_]:PTS[tsl_] + 128],
                                   AF.Identity))(wt_i, tsl),
                               waits=[("pe", c_wt)], inc="act")
                ptF_rel[tsl] = ("act", wT_done)
        prev_scan = s.op("vector",
                         (lambda m_: lambda e: e.tensor_tensor_scan(
                             acc[:, T * m_:T * (m_ + 1)], a_t[:], xs_t[:], 0.0,
                             ALU.mult, ALU.add))(m),
                         waits=[("dve", s.c("dve"))], inc="dve")
        scan_done = prev_scan
    attn_pe_done = s.c("pe")

    ctx_at.close()
    ctx_sc.close()
    ctx_moe = contextlib.ExitStack()
    h_sb = ctx_moe.enter_context(nc.sbuf_tensor("h_sb", [128, NH * 512], BF16))
    tmp_eo = ctx_moe.enter_context(nc.sbuf_tensor("tmp_eo", [128, 512], F32))
    fxbuf = [ctx_moe.enter_context(nc.sbuf_tensor(f"fxbuf{i}", [128, 512], F32)) for i in range(2)]
    out_tm = [ctx_moe.enter_context(nc.sbuf_tensor(f"out_tm{i}", [128, 512], F32)) for i in range(4)]
    fx_rel = {0: ("dve", 0), 1: ("dve", 0)}
    ot_rel = {0: ("ou0", 0), 1: ("ou1", 0), 2: ("ou2", 0), 3: ("ou3", 0)}
    acc_evac_nd = {}

    def fin_slot_ap(q):
        if q < 2:
            return ptF[:, PTS[q]:PTS[q] + 128]
        return p[2 + q][:, 0:128]

    def emit_final_half(i, hh, n_grp):
        if not hasattr(emit_final_half, "tslrel"):
            emit_final_half.tslrel = {0: ptF_rel[0], 1: ptF_rel[1],
                                      2: bank_rel[4], 3: bank_rel[5]}
        if True:
            xsl = (2 * i + hh) % 2
            xd_i = s.op("gpsimd",
                        (lambda i_, hh_, xsl_: lambda e: e.dma_start(
                            out=fxbuf[xsl_][:],
                            in_=X[128 * i_:128 * (i_ + 1), 512 * hh_:512 * (hh_ + 1)]))(i, hh, xsl),
                        waits=[fx_rel[xsl], ("dve", scan_done), ("pe", attn_pe_done)],
                        inc=f"fx{xsl}", inc_k=16)
            osl = (2 * i + hh) % 4
            last_add = 0
            for cc in range(4):
                c = 4 * hh + cc
                q = (i * KD + c) % 4
                rel = emit_final_half.tslrel[q]
                c_tp = s.op("tensor",
                            (lambda i_, c_, q_: lambda e: e.transpose(
                                fin_slot_ap(q_),
                                acc[:, T * c_ + 128 * i_:T * c_ + 128 * (i_ + 1)],
                                identf_sb[:]))(i, c, q),
                            waits=[acc_evac_nd[(n_grp, c)], rel], inc="pe")
                last_add = s.op("vector",
                                (lambda i_, c_, cc_, xsl_, osl_, q_: lambda e: e.tensor_add(
                                    out_tm[osl_][:, 128 * cc_:128 * (cc_ + 1)],
                                    fin_slot_ap(q_),
                                    fxbuf[xsl_][:, 128 * cc_:128 * (cc_ + 1)]))(i, c, cc, xsl, osl, q),
                                waits=[("pe", c_tp), (f"fx{xsl}", xd_i), ot_rel[osl]],
                                inc="dve")
                emit_final_half.tslrel[q] = ("dve", last_add)
            ousem = f"ou{osl}"
            oc = s.op("sync",
                      (lambda i_, hh_, osl_: lambda e: e.dma_start(
                          out=OUT[128 * i_:128 * (i_ + 1), 512 * hh_:512 * (hh_ + 1)],
                          in_=out_tm[osl_][:]))(i, hh, osl),
                      waits=[("dve", last_add)], inc=ousem, inc_k=16)
            ot_rel[osl] = (ousem, oc)
            fx_rel[xsl] = ("dve", last_add)

    w2_wd = {}
    mm1_done = {}
    mm2_done = {}
    gelu_cnt = {}
    acc_last_for_ntile = {n: ("dve", scan_done) for n in range(NT)}
    wb_rel = ("dve", 0)
    wrow_rel = ("pe", 0)
    last_wb_read = 0
    wb_done_act = 0
    last_tmp_read = 0

    n_idx = len(nexp_list) * 2
    for idx in range(n_idx):
        eidx = nexp_list[idx // 2]
        half = idx % 2
        if idx == 0:
            for kk in range(4, KD):
                castdma(
                    (lambda kk_: lambda: w1buf[:, kk_ * HALF:(kk_ + 1) * HALF])(kk),
                    (lambda kk_: lambda: w1_src(nexp_list[0], 0, kk_, kk_ + 1))(kk),
                    waits=[("pe", attn_pe_done)], sem=f"w1b{kk}")
            w1_wd[0] = s.c("w1d")
        else:
            w1_wd[idx] = castdma(
                lambda ei=eidx, h=half: w1buf[:].rearrange("p (k h2) -> p k h2", h2=HALF),
                lambda ei=eidx, h=half: w1_src(ei, h),
                waits=[("pe", mm1_done[idx - 1])], sem="w1d")
        w2_wd[idx] = castdma(
            lambda ei=eidx, h=half: w2buf[:].rearrange("p (k d2) -> p k d2", d2=D),
            lambda ei=eidx, h=half: w2_src(ei, h),
            waits=[("pe", attn_pe_done if idx == 0 else mm2_done[idx - 1])], sem="w2d")

        if half == 0 and eidx != "sh":
            c_wr = s.op("sync",
                        (lambda e_=eidx: lambda e: e.dma_start(
                            out=wrow0[:], in_=wT[e_:e_ + 1, :]))(),
                        waits=[("act", wT_done), wrow_rel], inc="wr", inc_k=16)
            c_bc = {}
            wb_cp = {}
            for n in range(NT):
                bk = 4 + n % 2
                c_bc[n] = s.op("tensor",
                               (lambda n_, bk_: lambda e: e.matmul(
                                   p[bk_][:], onesb_sb[:],
                                   wrow0[:, 512 * n_:512 * (n_ + 1)],
                                   start=True, stop=True))(n, bk),
                               waits=[("wr", c_wr), bank_rel[bk], wb_rel], inc="pe")
                wb_cp[n] = s.op("scalar",
                                (lambda n_, bk_: lambda e: e.activation(
                                    wb[:, 512 * n_:512 * (n_ + 1)], p[bk_][:],
                                    AF.Identity))(n, bk),
                                waits=[("pe", c_bc[n])], inc="act")
                bank_rel[bk] = ("act", wb_cp[n])
            wb_done_act = s.c("act")
            wrow_rel = ("pe", c_bc[NT - 1])

        mm1_last_pe = 0
        for n in range(NT):
            for mm in range(NH):
                bank = mm % 2
                c_mm = 0
                for k in range(KD):
                    if idx == 0:
                        wsem = (f"w1b{k}", 16) if k >= 4 else ("w1d", w1_wd[0])
                        xwaits = [wsem, bank_rel[bank], ("dve", scan_done)]
                    else:
                        xwaits = [("w1d", w1_wd[idx]), bank_rel[bank]]
                    c_mm = s.op("tensor",
                                (lambda k_, mm_, n_, b_: lambda e: e.matmul(
                                    p[b_][:],
                                    w1buf[:, HALF * k_ + 128 * mm_:HALF * k_ + 128 * (mm_ + 1)],
                                    xnT[:, T * k_ + 512 * n_:T * k_ + 512 * (n_ + 1)],
                                    start=(k_ == 0), stop=(k_ == KD - 1)))(k, mm, n, bank),
                                waits=xwaits, inc="pe")
                mm1_last_pe = c_mm
                bcol = (E * 32 + 16 * half + mm) if eidx == "sh" else (32 * eidx + 16 * half + mm)
                gwaits = [("pe", c_mm)]
                if idx == 0:
                    gwaits.append(("dve", scan_done))
                gelu_cnt[(idx, n, mm)] = s.op(
                    "scalar",
                    (lambda mm_, b_, bc_: lambda e: e.activation(
                        h_sb[:, 512 * mm_:512 * (mm_ + 1)], p[b_][:], AF.Gelu,
                        bias=b1all[:, bc_:bc_ + 1]))(mm, bank, bcol),
                    waits=gwaits, inc="act")
                bank_rel[bank] = ("act", gelu_cnt[(idx, n, mm)])
            for d in range(ND):
                bank = 2 + d % 2
                c_mm = 0
                sh_tail = (eidx == "sh" and half == 1)
                for k in range(NH):
                    is_stop = (k == NH - 1) and not sh_tail
                    c_mm = s.op("tensor",
                                (lambda k_, d_, n_, b_, st_: lambda e: e.matmul(
                                    p[b_][:],
                                    w2buf[:, D * k_ + 128 * d_:D * k_ + 128 * (d_ + 1)],
                                    h_sb[:, 512 * k_:512 * (k_ + 1)],
                                    start=(k_ == 0), stop=st_))(k, d, n, bank, is_stop),
                                waits=[("w2d", w2_wd[idx]),
                                       ("act", gelu_cnt[(idx, n, k)]),
                                       bank_rel[bank]], inc="pe")
                if sh_tail:
                    c_mm = s.op("tensor",
                                (lambda d_, n_, b_: lambda e: e.matmul(
                                    p[b_][:],
                                    b2sb[:, 128 * d_:128 * (d_ + 1)],
                                    wT[:, 512 * n_:512 * (n_ + 1)],
                                    start=False, stop=True))(d, n, bank),
                                waits=[("act", wT_done)], inc="pe")
                mm2_done_grp = c_mm
                prev = acc_last_for_ntile[n]
                if eidx == "sh":
                    if half == 0:
                        c_ev = s.op("vector",
                                    (lambda d_, n_, b_: lambda e: e.tensor_add(
                                        acc[:, T * d_ + 512 * n_:T * d_ + 512 * (n_ + 1)],
                                        acc[:, T * d_ + 512 * n_:T * d_ + 512 * (n_ + 1)],
                                        p[b_][:]))(d, n, bank),
                                    waits=[("pe", mm2_done_grp), prev], inc="dve")
                    else:
                        c_ev = s.op("vector",
                                    (lambda d_, n_, b_: lambda e: e.scalar_tensor_tensor(
                                        acc[:, T * d_ + 512 * n_:T * d_ + 512 * (n_ + 1)],
                                        p[b_][:], bs2_sb[:, d_:d_ + 1],
                                        acc[:, T * d_ + 512 * n_:T * d_ + 512 * (n_ + 1)],
                                        ALU.add, ALU.add))(d, n, bank),
                                    waits=[("pe", mm2_done_grp), prev], inc="dve")
                    bank_rel[bank] = ("dve", c_ev)
                else:
                    c_t = s.op("vector",
                               (lambda d_, n_, b_: lambda e: e.tensor_mul(
                                   tmp_eo[:], p[b_][:],
                                   wb[:, 512 * n_:512 * (n_ + 1)]))(d, n, bank),
                               waits=[("pe", mm2_done_grp), ("act", wb_done_act),
                                      ("dve", last_tmp_read)],
                               inc="dve")
                    bank_rel[bank] = ("dve", c_t)
                    last_wb_read = c_t
                    c_ev = s.op("vector",
                                (lambda d_, n_: lambda e: e.tensor_add(
                                    acc[:, T * d_ + 512 * n_:T * d_ + 512 * (n_ + 1)],
                                    acc[:, T * d_ + 512 * n_:T * d_ + 512 * (n_ + 1)],
                                    tmp_eo[:]))(d, n),
                                waits=[prev, ("dve", c_t)], inc="dve")
                    last_tmp_read = c_ev
                acc_last_for_ntile[n] = ("dve", c_ev)
                acc_evac_nd[(n, d)] = ("dve", c_ev)
                if idx == n_idx - 1 and n >= 1:
                    emit_final_half(4 * (n - 1) + d // 2, d % 2, n - 1)
                if idx == n_idx - 1 and n == NT - 1 and d == 3:
                    for ti in range(4):
                        emit_final_half(4 * n + ti, 0, n)
            if idx == n_idx - 1 and n == NT - 1:
                for ti in range(4):
                    emit_final_half(4 * n + ti, 1, n)
        mm1_done[idx] = mm1_last_pe
        mm2_done[idx] = s.c("pe")
        if half == 1 and eidx != "sh":
            wb_rel = ("dve", last_wb_read)

    s.op("sync", lambda e: e.nop(),
         waits=[("ou0", s.c("ou0")), ("ou1", s.c("ou1")),
                ("ou2", s.c("ou2")), ("ou3", s.c("ou3"))])
    if debug:
        dbg_list = [("d_wT", wT, BF16, [E, T]),
                    ("d_wb", wb, BF16, [128, T]),
                    ("d_bgc", bgate_bc, F32, [128, E]),
                    ("d_xnT", xnT, BF16, [128, KD * T]),
                    ("d_acc", acc, F32, [128, KD * T])]
        dl = 0
        for nm, t_, dty, shp in dbg_list:
            od = nc.dram_tensor(nm, shp, dty, kind="ExternalOutput").ap()
            dl = s.op("sync",
                      (lambda od_, t2: lambda e: e.dma_start(out=od_[:, :], in_=t2[:]))(od, t_),
                      waits=[("ou0", s.c("ou0")), ("ou1", s.c("ou1")),
                             ("ou2", s.c("ou2")), ("ou3", s.c("ou3"))],
                      inc="wr", inc_k=16)
        s.op("sync", lambda e: e.nop(), waits=[("wr", dl)])

    with nc.Block() as block:
        s.emit(block, sems)

    n_instr = sum(len(v) for v in s.ops.values())
    ctx_moe.close()
    ctx.close()
    return nc, n_instr


def host_inputs(inputs):
    f = np.ascontiguousarray
    base = {
        "awg": f(inputs["L0_attn_Wg"]),
        "awv": f(inputs["L0_attn_Wv"]),
        "awd": f(inputs["L0_attn_Wd"]),
        "battn": f(np.stack([
            inputs["L0_attn_bg"].reshape(KD, 128).T,
            inputs["L0_attn_bv"].reshape(KD, 128).T,
            inputs["L0_attn_bd"].reshape(KD, 128).T])),
        "wmoet": f(inputs["L0_ffn_Wg"].T),
        "bgr1": f(inputs["L0_ffn_bg"].reshape(1, E)),
        "bgr2": f(inputs["L0_ffn_bias"].reshape(1, E)),
        "w1t": f(inputs["L0_ffn_W1"].transpose(0, 2, 1)),
        "b1p": f(inputs["L0_ffn_b1"].reshape(E, 32, 128).transpose(0, 2, 1)),
        "w2t": f(inputs["L0_ffn_W2"].transpose(0, 2, 1)),
        "b2n": f(inputs["L0_ffn_b2"]),
        "ws1t": f(inputs["L0_ffn_Ws1"].T),
        "bs1p": f(inputs["L0_ffn_bs1"].reshape(32, 128).T),
        "ws2t": f(inputs["L0_ffn_Ws2"].T),
        "bs2p": f(inputs["L0_ffn_bs2"].reshape(KD, 128).T),
        "identf": np.eye(128, dtype=np.float32),
        "onesb": np.ones((1, 128), dtype=ml_dtypes.bfloat16),
        "onesf": np.ones((1, 128), dtype=np.float32),
    }
    x = np.asarray(inputs["L0_attn_x"], dtype=np.float32)
    maps = []
    for c in range(8):
        m = dict(base)
        m["x"] = f(x[c])
        maps.append(m)
    return maps


def kernel(**inputs):
    nc, _ = build_program(nexp=8)
    in_maps = host_inputs(inputs)
    res = run_bass_kernel_spmd(nc, in_maps, core_ids=list(range(8)))
    return np.stack([r["out"] for r in res.results], axis=0)


if __name__ == "__main__":
    import sys
    nexp = int(sys.argv[1]) if len(sys.argv) > 1 else 1
    rng = np.random.default_rng(0)
    sc = 0.02
    inputs = {
        "L0_attn_x": rng.standard_normal((8, T, D)).astype(np.float32),
        "L0_attn_Wg": (rng.standard_normal((D, D)) / np.sqrt(D)).astype(np.float32),
        "L0_attn_bg": (rng.standard_normal(D) * 0.1).astype(np.float32),
        "L0_attn_Wv": (rng.standard_normal((D, D)) / np.sqrt(D)).astype(np.float32),
        "L0_attn_bv": (rng.standard_normal(D) * 0.1).astype(np.float32),
        "L0_attn_Wd": (rng.standard_normal((D, D)) / np.sqrt(D)).astype(np.float32),
        "L0_attn_bd": (rng.standard_normal(D) * 0.5 + 1).astype(np.float32),
        "L0_ffn_Wg": (rng.standard_normal((E, D)) * sc).astype(np.float32),
        "L0_ffn_bg": (rng.standard_normal(E) * 0.1).astype(np.float32),
        "L0_ffn_bias": (rng.standard_normal(E) * 0.1).astype(np.float32),
        "L0_ffn_W1": (rng.standard_normal((E, H, D)) * sc).astype(np.float32),
        "L0_ffn_b1": (rng.standard_normal((E, H)) * 0.05).astype(np.float32),
        "L0_ffn_W2": (rng.standard_normal((E, D, H)) * sc).astype(np.float32),
        "L0_ffn_b2": (rng.standard_normal((E, D)) * 0.05).astype(np.float32),
        "L0_ffn_Ws1": (rng.standard_normal((H, D)) * sc).astype(np.float32),
        "L0_ffn_bs1": (rng.standard_normal(H) * 0.05).astype(np.float32),
        "L0_ffn_Ws2": (rng.standard_normal((D, H)) * sc).astype(np.float32),
        "L0_ffn_bs2": (rng.standard_normal(D) * 0.05).astype(np.float32),
    }

    def np_ref(c, nexp):
        from scipy.special import erf
        x = inputs["L0_attn_x"][c].astype(np.float64)
        m = x.mean(1, keepdims=True); v = x.var(1, keepdims=True)
        xn = (x - m) / np.sqrt(v + 1e-5)
        g = xn @ inputs["L0_attn_Wg"] + inputs["L0_attn_bg"]
        vv = xn @ inputs["L0_attn_Wv"] + inputs["L0_attn_bv"]
        d = xn @ inputs["L0_attn_Wd"] + inputs["L0_attn_bd"]
        sig = lambda z: 1 / (1 + np.exp(-z))
        xs = sig(g) * np.tanh(vv)
        a = 0.001 + 0.998 * sig(d)
        h = np.zeros_like(xs); st = np.zeros(D)
        for t in range(T):
            st = a[t] * st + xs[t]
            h[t] = st
        gelu = lambda z: 0.5 * z * (1 + erf(z / np.sqrt(2)))
        scores = sig(xn @ inputs["L0_ffn_Wg"].T + inputs["L0_ffn_bg"] + inputs["L0_ffn_bias"])
        idx = np.argsort(-scores, 1, kind="stable")[:, :2]
        w = np.zeros_like(scores)
        np.put_along_axis(w, idx, np.take_along_axis(scores, idx, 1), 1)
        comb = np.zeros_like(xn)
        for e in range(nexp):
            he = gelu(xn @ inputs["L0_ffn_W1"][e].T + inputs["L0_ffn_b1"][e])
            eo = he @ inputs["L0_ffn_W2"][e].T
            comb += w[:, e:e + 1] * eo
        comb += w[:, :] @ inputs["L0_ffn_b2"]
        shared = gelu(xn @ inputs["L0_ffn_Ws1"].T + inputs["L0_ffn_bs1"]) @ \
            inputs["L0_ffn_Ws2"].T + inputs["L0_ffn_bs2"]
        return x + h + shared + comb

    nc, n_instr = build_program(nexp=nexp)
    print("instructions:", n_instr)
    in_maps = host_inputs(inputs)
    res = run_bass_kernel_spmd(nc, in_maps[:1], core_ids=[0])
    got = res.results[0]["out"]
    ref = np_ref(0, nexp)
    err = np.linalg.norm(got - ref) / np.linalg.norm(ref)
    print("rel l2:", err)
    rvr = float(np.mean((got - ref) ** 2) / np.mean(ref ** 2))
    print("rvr:", rvr)


# version 17
# speedup vs baseline: 1.0057x; 1.0018x over previous
import contextlib

import numpy as np
import ml_dtypes

import concourse.bass as bass
import concourse.mybir as mybir
from concourse.bass_utils import run_bass_kernel_spmd

F32 = mybir.dt.float32
BF16 = mybir.dt.bfloat16
AF = mybir.ActivationFunctionType
ALU = mybir.AluOpType
AX = mybir.AxisListType

D = 1024
H = 4096
E = 8
T = 2048
KD = D // 128
TT = T // 128
NT = T // 512
NH = 16
ND = 8
HALF = 2048


class Sched:

    ENGINES = ("tensor", "scalar", "vector", "gpsimd", "sync")

    def __init__(self):
        self.ops = {e: [] for e in self.ENGINES}
        self.cnt = {}
        self.last_wait = {e: {} for e in self.ENGINES}

    def op(self, eng, fn, waits=(), inc=None, inc_k=1):
        w = []
        for sem, val in waits:
            if val <= 0:
                continue
            if self.last_wait[eng].get(sem, -1) >= val:
                continue
            self.last_wait[eng][sem] = val
            w.append((sem, val))
        if inc is not None:
            self.cnt[inc] = self.cnt.get(inc, 0) + inc_k
        self.ops[eng].append((fn, w, inc, inc_k))
        return self.cnt.get(inc, 0)

    def c(self, sem):
        return self.cnt.get(sem, 0)

    def emit(self, block, sems):
        def mk(eng_name):
            def body(e):
                for fn, waits, inc, inc_k in self.ops[eng_name]:
                    for sem, val in waits:
                        e.wait_ge(sems[sem], val)
                    instr = fn(e)
                    if inc is not None:
                        instr.then_inc(sems[inc], inc_k)
            return body
        for eng_name in self.ENGINES:
            if self.ops[eng_name]:
                getattr(block, eng_name)(mk(eng_name))


def build_program(nexp=8, debug=False):
    nc = bass.Bass("TRN2", target_bir_lowering=False)

    dt = lambda n, s: nc.dram_tensor(n, s, F32, kind="ExternalInput").ap()
    X = dt("x", [T, D])
    AWG = dt("awg", [D, D]); AWV = dt("awv", [D, D]); AWD = dt("awd", [D, D])
    BATTN = dt("battn", [3, 128, KD])
    WMOET = dt("wmoet", [D, E])
    BGR1 = dt("bgr1", [1, E]); BGR2 = dt("bgr2", [1, E])
    W1T = dt("w1t", [E, D, H])
    B1P = dt("b1p", [E, 128, 32])
    W2T = dt("w2t", [E, H, D])
    B2N = dt("b2n", [E, D])
    WS1T = dt("ws1t", [D, H])
    BS1P = dt("bs1p", [128, 32])
    WS2T = dt("ws2t", [H, D])
    BS2P = dt("bs2p", [128, KD])
    IDENTF = nc.dram_tensor("identf", [128, 128], F32, kind="ExternalInput").ap()
    ONESB = nc.dram_tensor("onesb", [1, 128], BF16, kind="ExternalInput").ap()
    ONESF = nc.dram_tensor("onesf", [1, 128], F32, kind="ExternalInput").ap()
    OUT = nc.dram_tensor("out", [T, D], F32, kind="ExternalOutput").ap()

    s = Sched()
    ctx = contextlib.ExitStack()
    sb = lambda n, shape, dty: ctx.enter_context(nc.sbuf_tensor(n, shape, dty))

    xnT = sb("xnT", [128, KD * T], BF16)
    acc = sb("acc", [128, KD * T], F32)
    w1buf = sb("w1buf", [128, KD * HALF], BF16)
    w2buf = sb("w2buf", [128, NH * D], BF16)
    wT = sb("wT", [E, T], BF16)
    b2sb = sb("b2sb", [E, D], BF16)
    wmoe_sb = sb("wmoe_sb", [128, KD * E], BF16)
    b1all = sb("b1all", [128, E * 32 + 32], F32)
    battn_sb = sb("battn_sb", [128, 3 * KD], F32)
    bs2_sb = sb("bs2_sb", [128, KD], F32)
    bgr1_sb = sb("bgr1_sb", [1, E], F32)
    bgr2_sb = sb("bgr2_sb", [1, E], F32)
    bgrow = sb("bgrow", [1, E], F32)
    bgate_bc = sb("bgate_bc", [128, E], F32)
    identf_sb = sb("identf_sb", [128, 128], F32)
    onesb_sb = sb("onesb_sb", [1, 128], BF16)
    onesf_sb = sb("onesf_sb", [1, 128], F32)
    eps_sb = sb("eps_sb", [128, 1], F32)
    stats = sb("stats", [128, 6 * TT], F32)
    wrow0 = sb("wrow0", [1, T], BF16)
    wb = sb("wb", [128, T], BF16)

    p = [ctx.enter_context(nc.psum_tensor(f"p{i}", [128, 512], F32)) for i in range(6)]
    ptF = ctx.enter_context(nc.psum_tensor("ptF", [128, 1024], F32))
    PTS = (0, 512)

    sem_names = ["cst", "xl0", "xl1", "xl2", "xl3", "wr",
                 "fx0", "fx1", "ou0", "ou1", "ou2", "ou3",
                 "wdc", "wda", "w1d", "w2d", "pe", "act", "dve",
                 "w1b4", "w1b5", "w1b6", "w1b7"]
    sems = {n: ctx.enter_context(nc.semaphore(n)) for n in sem_names}

    bank_rel = {i: ("pe", 0) for i in range(6)}
    ptF_rel = {0: ("pe", 0), 1: ("pe", 0)}

    def syncdma(dst_ap_fn, src_ap_fn, sem="cst", waits=(), eng="scalar"):
        return s.op(eng, lambda e: e.dma_start(out=dst_ap_fn(), in_=src_ap_fn()),
                    waits=waits, inc=sem, inc_k=16)

    syncdma(lambda: identf_sb[:], lambda: IDENTF[:, :])
    syncdma(lambda: onesb_sb[:], lambda: ONESB[:, :])
    syncdma(lambda: onesf_sb[:], lambda: ONESF[:, :])
    syncdma(lambda: battn_sb[:].rearrange("p (j c) -> p j c", c=KD),
            lambda: BATTN.rearrange("j p c -> p j c"))
    syncdma(lambda: b1all[:, 0:E * 32].rearrange("p (e c) -> p e c", c=32),
            lambda: B1P.rearrange("e p c -> p e c"))
    syncdma(lambda: b1all[:, E * 32:E * 32 + 32], lambda: BS1P[:, :])
    syncdma(lambda: bs2_sb[:], lambda: BS2P[:, :])
    syncdma(lambda: bgr1_sb[:], lambda: BGR1[:, :])
    syncdma(lambda: bgr2_sb[:], lambda: BGR2[:, :])
    consts_xd = s.c("cst")

    def castdma(dst_ap_fn, src_ap_fn, waits=(), sem="wdc"):
        return s.op("gpsimd", lambda e: e.dma_start(out=dst_ap_fn(), in_=src_ap_fn()),
                    waits=waits, inc=sem, inc_k=16)

    castdma(lambda: b2sb[:], lambda: B2N[:, :])
    castdma(lambda: wmoe_sb[:].rearrange("p (k e) -> p k e", e=E),
            lambda: WMOET.rearrange("(k p) e -> p k e", p=128))
    wmoe_wdc = s.c("wdc")
    castdma(lambda: w2buf[:, 0:KD * D].rearrange("p (k e) -> p k e", e=D),
            lambda: AWG.rearrange("(k p) e -> p k e", p=128), sem="wda")
    castdma(lambda: w2buf[:, KD * D:2 * KD * D].rearrange("p (k e) -> p k e", e=D),
            lambda: AWV.rearrange("(k p) e -> p k e", p=128), sem="wda")

    ctx_sc = contextlib.ExitStack()
    sc_tm = ctx_sc.enter_context(nc.sbuf_tensor("sc_tm", [128, E * TT], F32))
    sig_tm = ctx_sc.enter_context(nc.sbuf_tensor("sig_tm", [128, E * TT], F32))
    srt_tm = ctx_sc.enter_context(nc.sbuf_tensor("srt_tm", [128, E * TT], F32))
    negm2_t = ctx_sc.enter_context(nc.sbuf_tensor("negm2_t", [128, TT], F32))
    ctx_ln = contextlib.ExitStack()
    xbuf = [ctx_ln.enter_context(nc.sbuf_tensor(f"xbuf{i}", [128, D], F32)) for i in range(4)]
    xnb = [ctx_ln.enter_context(nc.sbuf_tensor(f"xnb{i}", [128, D], F32)) for i in range(2)]
    g_t = ctx_ln.enter_context(nc.sbuf_tensor("g_t", [128, D], F32))

    s.op("vector", lambda e: e.memset(eps_sb[:], 1e-5), inc="dve")

    c_bg = s.op("vector", lambda e: e.tensor_add(bgrow[:], bgr1_sb[:], bgr2_sb[:]),
                waits=[("cst", consts_xd)], inc="dve")
    c_bgmm = s.op("tensor",
                  lambda e: e.matmul(p[4][:, 0:E], onesf_sb[:], bgrow[:],
                                     start=True, stop=True),
                  waits=[("dve", c_bg), bank_rel[4]], inc="pe")
    c_bgc = s.op("vector", lambda e: e.tensor_copy(bgate_bc[:], p[4][:, 0:E]),
                 waits=[("pe", c_bgmm)], inc="dve")
    bank_rel[4] = ("dve", c_bgc)

    xl_cnt = {}
    c_red = {}; c_sqr = {}; c_vn = {}; c_ex = {}; c_b2 = {}; c_id = {}
    cp_done = {}; sc_add = {}
    sc_slot_rel = {i: ("dve", 0) for i in range(4)}
    xrel = {}
    xnb_rel = {0: ("pe", 0), 1: ("pe", 0)}

    def st(col, i):
        return stats[:, col * TT + i:col * TT + i + 1]

    def ln_stage_a(i):
        sl4 = i % 4
        w = [xrel[sl4]] if sl4 in xrel else []
        xl_cnt[i] = s.op("sync",
                         (lambda i_, sl_: lambda e: e.dma_start(
                             out=xbuf[sl_][:], in_=X[128 * i_:128 * (i_ + 1), :]))(i, sl4),
                         waits=w, inc=f"xl{sl4}", inc_k=16)
        c_red[i] = s.op("vector",
                        (lambda i_, sl_: lambda e: e.reduce_sum(
                            st(0, i_), xbuf[sl_][:], axis=AX.X))(i, sl4),
                        waits=[(f"xl{i % 4}", xl_cnt[i])], inc="dve")
        c_sqr[i] = s.op("scalar",
                        (lambda i_, sl_: lambda e: e.activation(
                            g_t[:], xbuf[sl_][:], AF.Square,
                            accum_out=st(2, i_)))(i, sl4),
                        waits=[(f"xl{i % 4}", xl_cnt[i]),
                               ("act", c_sqr.get(i - 1, 0))], inc="act")

    def ln_stage_b1(i):
        c_m = s.op("vector",
                   (lambda i_: lambda e: e.tensor_mul(st(1, i_), st(0, i_), st(0, i_)))(i),
                   waits=[("dve", c_red[i])], inc="dve")
        c_vn[i] = s.op("vector",
                       (lambda i_: lambda e: e.scalar_tensor_tensor(
                           st(3, i_), st(1, i_), -1.0 / D, st(2, i_),
                           ALU.mult, ALU.add))(i),
                       waits=[("dve", c_m), ("act", c_sqr[i])], inc="dve")

    def ln_stage_b2(i):
        c_l = s.op("scalar",
                   (lambda i_: lambda e: e.activation(
                       st(4, i_), st(3, i_), AF.Ln,
                       bias=eps_sb[:, 0:1], scale=1.0 / D))(i),
                   waits=[("dve", c_vn[i])], inc="act")
        c_ex[i] = s.op("scalar",
                       (lambda i_: lambda e: e.activation(
                           st(4, i_), st(4, i_), AF.Exp, scale=-0.5))(i),
                       waits=[("act", c_l)], inc="act")

    def ln_stage_b3(i):
        sl4 = i % 4
        sl2 = i % 2
        c_b2[i] = s.op("vector",
                       (lambda i_: lambda e: e.scalar_tensor_tensor(
                           st(5, i_), st(0, i_), -1.0 / D, st(4, i_),
                           ALU.mult, ALU.mult))(i),
                       waits=[("act", c_ex[i])], inc="dve")
        c_id[i] = s.op("scalar",
                       (lambda i_, sl_, sl2_: lambda e: e.activation(
                           xnb[sl2_][:], xbuf[sl_][:], AF.Identity,
                           bias=st(5, i_), scale=st(4, i_)))(i, sl4, sl2),
                       waits=[("dve", c_b2[i]), xnb_rel[sl2]], inc="act")
        xrel[sl4] = ("act", c_id[i])

    def ln_stage_c(i):
        sl2 = i % 2
        last_tp = 0
        last_cp = 0
        for c in range(KD):
            tsl = (i * KD + c) % 2
            last_tp = s.op("tensor",
                           (lambda sl2_, c_, tsl_: lambda e: e.transpose(
                               ptF[:, PTS[tsl_]:PTS[tsl_] + 128],
                               xnb[sl2_][:, 128 * c_:128 * (c_ + 1)],
                               identf_sb[:]))(sl2, c, tsl),
                           waits=[("act", c_id[i]), ptF_rel[tsl], ("cst", consts_xd)],
                           inc="pe")
            last_cp = s.op("vector",
                           (lambda i_, c_, tsl_: lambda e: e.tensor_copy(
                               xnT[:, T * c_ + 128 * i_:T * c_ + 128 * (i_ + 1)],
                               ptF[:, PTS[tsl_]:PTS[tsl_] + 128]))(i, c, tsl),
                           waits=[("pe", last_tp)], inc="dve")
            ptF_rel[tsl] = ("dve", last_cp)
        xnb_rel[sl2] = ("pe", last_tp)
        cp_done[i] = last_cp
        slot = i % 4
        c_mm = 0
        for k in range(KD):
            c_mm = s.op("tensor",
                        (lambda i_, k_, slot_: lambda e: e.matmul(
                            p[slot_][:, 0:E],
                            xnT[:, T * k_ + 128 * i_:T * k_ + 128 * (i_ + 1)],
                            wmoe_sb[:, E * k_:E * (k_ + 1)],
                            start=(k_ == 0), stop=(k_ == KD - 1)))(i, k, slot),
                        waits=[("dve", last_cp), ("wdc", wmoe_wdc),
                               sc_slot_rel[slot]],
                        inc="pe" if k == KD - 1 else None)
        sc_add[i] = s.op("vector",
                         (lambda i_, slot_: lambda e: e.tensor_add(
                             sc_tm[:, E * i_:E * (i_ + 1)], p[slot_][:, 0:E],
                             bgate_bc[:]))(i, slot),
                         waits=[("pe", c_mm), ("dve", c_bgc)], inc="dve")
        sc_slot_rel[slot] = ("dve", sc_add[i])

    for i in range(TT + 4):
        if i < TT:
            ln_stage_a(i)
        if 1 <= i < TT + 1:
            ln_stage_b1(i - 1)
        if 2 <= i < TT + 2:
            ln_stage_b2(i - 2)
        if 3 <= i < TT + 3:
            ln_stage_b3(i - 3)
        if 4 <= i:
            ln_stage_c(i - 4)
    xnT_done_dve = s.c("dve")

    sig_done = {}
    for i in range(TT):
        sig_done[i] = s.op("scalar",
                           (lambda i_: lambda e: e.activation(
                               sig_tm[:, E * i_:E * (i_ + 1)],
                               sc_tm[:, E * i_:E * (i_ + 1)], AF.Sigmoid))(i),
                           waits=[("dve", sc_add[i])], inc="act")
    negm2_done = {}
    for i in range(TT):
        c_srt = s.op("vector",
                     (lambda i_: lambda e: e.max(
                         srt_tm[:, E * i_:E * (i_ + 1)], sig_tm[:, E * i_:E * (i_ + 1)]))(i),
                     waits=[("act", sig_done[i])], inc="dve")
        negm2_done[i] = s.op("vector",
                             (lambda i_: lambda e: e.tensor_scalar_mul(
                                 negm2_t[:, i_:i_ + 1],
                                 srt_tm[:, E * i_ + 1:E * i_ + 2], -1.0))(i),
                             waits=[("dve", c_srt)], inc="dve")
    sgn_done = {}
    for i in range(TT):
        sgn_done[i] = s.op("scalar",
                           (lambda i_: lambda e: e.activation(
                               sc_tm[:, E * i_:E * (i_ + 1)],
                               sig_tm[:, E * i_:E * (i_ + 1)], AF.Sign,
                               bias=negm2_t[:, i_:i_ + 1]))(i),
                           waits=[("dve", negm2_done[i])], inc="act")
    c_w = {}
    for i in range(TT):
        c_msk = s.op("vector",
                     (lambda i_: lambda e: e.tensor_scalar(
                         sc_tm[:, E * i_:E * (i_ + 1)],
                         sc_tm[:, E * i_:E * (i_ + 1)], 1.0, 1.0, ALU.add, ALU.min))(i),
                     waits=[("act", sgn_done[i])], inc="dve")
        c_w[i] = s.op("vector",
                      (lambda i_: lambda e: e.tensor_mul(
                          sig_tm[:, E * i_:E * (i_ + 1)],
                          sc_tm[:, E * i_:E * (i_ + 1)],
                          sig_tm[:, E * i_:E * (i_ + 1)]))(i),
                      waits=[("dve", c_msk)], inc="dve")

    ctx_ln.close()
    ctx_at = contextlib.ExitStack()
    xs_t = ctx_at.enter_context(nc.sbuf_tensor("xs_t", [128, T], F32))
    a_t = ctx_at.enter_context(nc.sbuf_tensor("a_t", [128, T], F32))
    tscr_v = [ctx_at.enter_context(nc.sbuf_tensor(f"tscr_v{i}", [128, 512], F32)) for i in range(2)]
    wd_bf = w1buf

    castdma(lambda: wd_bf[:, KD * HALF // 2:].rearrange("p (k e) -> p k e", e=D),
            lambda: AWD.rearrange("(k p) e -> p k e", p=128), sem="wda")
    attnw_wd = s.c("wda")
    nexp_list = list(range(nexp)) + ["sh"]

    def w1_src(eidx, half, k0=0, k1=KD):
        if eidx == "sh":
            w = WS1T
        else:
            w = W1T[eidx]
        return w[128 * k0:128 * k1, HALF * half:HALF * (half + 1)].rearrange(
            "(k p) h -> p k h", p=128)

    def w2_src(eidx, half):
        if eidx == "sh":
            return WS2T[HALF * half:HALF * (half + 1), :].rearrange("(k p) d -> p k d", p=128)
        return W2T[eidx, HALF * half:HALF * (half + 1), :].rearrange("(k p) d -> p k d", p=128)

    w1_wd = {}
    castdma(lambda: w1buf[:, 0:4 * HALF].rearrange("p (k h) -> p k h", h=HALF),
            lambda: w1_src(nexp_list[0], 0, 0, 4), sem="w1d")

    tv_rel = {0: ("dve", 0), 1: ("dve", 0)}
    scan_done = 0
    xs_w_act = {}
    prev_scan = 0
    for m in range(KD):
        banks = (0, 1, 2) if m % 2 == 0 else (3, 4, 5)
        for n in range(NT):
            vsl = n % 2
            projs = [
                ("g", banks[0]),
                ("v", banks[1]),
                ("d", banks[2]),
            ]
            cm = {}
            for pi, (pn, bank) in enumerate(projs):
                c_mm = 0
                for k in range(KD):
                    if pn == "g":
                        lhs_fn = (lambda k_, m_: lambda: w2buf[:, D * k_ + 128 * m_:D * k_ + 128 * (m_ + 1)])(k, m)
                    elif pn == "v":
                        lhs_fn = (lambda k_, m_: lambda: w2buf[:, KD * D + D * k_ + 128 * m_:KD * D + D * k_ + 128 * (m_ + 1)])(k, m)
                    else:
                        lhs_fn = (lambda k_, m_: lambda: wd_bf[:, 8192 + D * k_ + 128 * m_:8192 + D * k_ + 128 * (m_ + 1)])(k, m)
                    c_mm = s.op("tensor",
                                (lambda lf, k_, n_, bank_: lambda e: e.matmul(
                                    p[bank_][:],
                                    lf(),
                                    xnT[:, T * k_ + 512 * n_:T * k_ + 512 * (n_ + 1)],
                                    start=(k_ == 0), stop=(k_ == KD - 1)))(lhs_fn, k, n, bank),
                                waits=[("wda", attnw_wd), ("dve", xnT_done_dve),
                                       bank_rel[bank]],
                                inc="pe" if k == KD - 1 else None)
                cm[pn] = c_mm
            c_sg = s.op("scalar",
                        (lambda m_, n_, b_: lambda e: e.activation(
                            xs_t[:, 512 * n_:512 * (n_ + 1)], p[b_][:], AF.Sigmoid,
                            bias=battn_sb[:, 0 * KD + m_:0 * KD + m_ + 1]))(m, n, banks[0]),
                        waits=[("pe", cm["g"]), ("dve", prev_scan)], inc="act")
            bank_rel[banks[0]] = ("act", c_sg)
            c_tv = s.op("scalar",
                        (lambda m_, n_, b_, vs_: lambda e: e.activation(
                            tscr_v[vs_][:], p[b_][:], AF.Tanh,
                            bias=battn_sb[:, 1 * KD + m_:1 * KD + m_ + 1]))(m, n, banks[1], vsl),
                        waits=[("pe", cm["v"]), tv_rel[vsl]], inc="act")
            bank_rel[banks[1]] = ("act", c_tv)
            c_sd = s.op("scalar",
                        (lambda m_, n_, b_: lambda e: e.activation(
                            a_t[:, 512 * n_:512 * (n_ + 1)], p[b_][:], AF.Sigmoid,
                            bias=battn_sb[:, 2 * KD + m_:2 * KD + m_ + 1]))(m, n, banks[2]),
                        waits=[("pe", cm["d"]), ("dve", prev_scan)], inc="act")
            bank_rel[banks[2]] = ("act", c_sd)
            c_xs = s.op("vector",
                        (lambda n_, vs_: lambda e: e.tensor_mul(
                            xs_t[:, 512 * n_:512 * (n_ + 1)],
                            xs_t[:, 512 * n_:512 * (n_ + 1)], tscr_v[vs_][:]))(n, vsl),
                        waits=[("act", c_sg), ("act", c_tv)], inc="dve")
            tv_rel[vsl] = ("dve", c_xs)
            c_a = s.op("vector",
                       (lambda n_: lambda e: e.tensor_scalar(
                           a_t[:, 512 * n_:512 * (n_ + 1)],
                           a_t[:, 512 * n_:512 * (n_ + 1)], 0.998, 0.001,
                           ALU.mult, ALU.add))(n),
                       waits=[("act", c_sd)], inc="dve")
            wt_i = m * NT + n
            if wt_i < TT:
                tsl = wt_i % 2
                c_wt = s.op("tensor",
                            (lambda i_, tsl_: lambda e: e.transpose(
                                ptF[0:E, PTS[tsl_]:PTS[tsl_] + 128],
                                sig_tm[:, E * i_:E * (i_ + 1)], identf_sb[:]))(wt_i, tsl),
                            waits=[("dve", c_w[wt_i]), ptF_rel[tsl]], inc="pe")
                wT_done = s.op("vector",
                               (lambda i_, tsl_: lambda e: e.tensor_copy(
                                   wT[0:E, 128 * i_:128 * (i_ + 1)],
                                   ptF[0:E, PTS[tsl_]:PTS[tsl_] + 128]))(wt_i, tsl),
                               waits=[("pe", c_wt)], inc="dve")
                ptF_rel[tsl] = ("dve", wT_done)
        prev_scan = s.op("vector",
                         (lambda m_: lambda e: e.tensor_tensor_scan(
                             acc[:, T * m_:T * (m_ + 1)], a_t[:], xs_t[:], 0.0,
                             ALU.mult, ALU.add))(m),
                         waits=[("dve", s.c("dve"))], inc="dve")
        scan_done = prev_scan
    attn_pe_done = s.c("pe")

    ctx_at.close()
    ctx_sc.close()
    ctx_moe = contextlib.ExitStack()
    h_sb = ctx_moe.enter_context(nc.sbuf_tensor("h_sb", [128, NH * 512], BF16))
    tmp_eo = ctx_moe.enter_context(nc.sbuf_tensor("tmp_eo", [128, 512], F32))
    fxbuf = [ctx_moe.enter_context(nc.sbuf_tensor(f"fxbuf{i}", [128, 512], F32)) for i in range(2)]
    out_tm = [ctx_moe.enter_context(nc.sbuf_tensor(f"out_tm{i}", [128, 512], F32)) for i in range(4)]
    fx_rel = {0: ("dve", 0), 1: ("dve", 0)}
    ot_rel = {0: ("ou0", 0), 1: ("ou1", 0), 2: ("ou2", 0), 3: ("ou3", 0)}
    acc_evac_nd = {}

    def fin_slot_ap(q):
        if q < 2:
            return ptF[:, PTS[q]:PTS[q] + 128]
        return p[2 + q][:, 0:128]

    def emit_final_half(i, hh, n_grp):
        if not hasattr(emit_final_half, "tslrel"):
            emit_final_half.tslrel = {0: ptF_rel[0], 1: ptF_rel[1],
                                      2: bank_rel[4], 3: bank_rel[5]}
        if True:
            xsl = (2 * i + hh) % 2
            xd_i = s.op("gpsimd",
                        (lambda i_, hh_, xsl_: lambda e: e.dma_start(
                            out=fxbuf[xsl_][:],
                            in_=X[128 * i_:128 * (i_ + 1), 512 * hh_:512 * (hh_ + 1)]))(i, hh, xsl),
                        waits=[fx_rel[xsl], ("dve", scan_done), ("pe", attn_pe_done)],
                        inc=f"fx{xsl}", inc_k=16)
            osl = (2 * i + hh) % 4
            last_add = 0
            for cc in range(4):
                c = 4 * hh + cc
                q = (i * KD + c) % 4
                rel = emit_final_half.tslrel[q]
                c_tp = s.op("tensor",
                            (lambda i_, c_, q_: lambda e: e.transpose(
                                fin_slot_ap(q_),
                                acc[:, T * c_ + 128 * i_:T * c_ + 128 * (i_ + 1)],
                                identf_sb[:]))(i, c, q),
                            waits=[acc_evac_nd[(n_grp, c)], rel], inc="pe")
                last_add = s.op("vector",
                                (lambda i_, c_, cc_, xsl_, osl_, q_: lambda e: e.tensor_add(
                                    out_tm[osl_][:, 128 * cc_:128 * (cc_ + 1)],
                                    fin_slot_ap(q_),
                                    fxbuf[xsl_][:, 128 * cc_:128 * (cc_ + 1)]))(i, c, cc, xsl, osl, q),
                                waits=[("pe", c_tp), (f"fx{xsl}", xd_i), ot_rel[osl]],
                                inc="dve")
                emit_final_half.tslrel[q] = ("dve", last_add)
            ousem = f"ou{osl}"
            oc = s.op("sync",
                      (lambda i_, hh_, osl_: lambda e: e.dma_start(
                          out=OUT[128 * i_:128 * (i_ + 1), 512 * hh_:512 * (hh_ + 1)],
                          in_=out_tm[osl_][:]))(i, hh, osl),
                      waits=[("dve", last_add)], inc=ousem, inc_k=16)
            ot_rel[osl] = (ousem, oc)
            fx_rel[xsl] = ("dve", last_add)

    w2_wd = {}
    mm1_done = {}
    mm2_done = {}
    gelu_cnt = {}
    acc_last_for_ntile = {n: ("dve", scan_done) for n in range(NT)}
    wb_rel = ("dve", 0)
    wrow_rel = ("pe", 0)
    last_wb_read = 0
    wb_done_act = 0
    last_tmp_read = 0

    n_idx = len(nexp_list) * 2
    for idx in range(n_idx):
        eidx = nexp_list[idx // 2]
        half = idx % 2
        if idx == 0:
            for kk in range(4, KD):
                castdma(
                    (lambda kk_: lambda: w1buf[:, kk_ * HALF:(kk_ + 1) * HALF])(kk),
                    (lambda kk_: lambda: w1_src(nexp_list[0], 0, kk_, kk_ + 1))(kk),
                    waits=[("pe", attn_pe_done)], sem=f"w1b{kk}")
            w1_wd[0] = s.c("w1d")
        else:
            w1_wd[idx] = castdma(
                lambda ei=eidx, h=half: w1buf[:].rearrange("p (k h2) -> p k h2", h2=HALF),
                lambda ei=eidx, h=half: w1_src(ei, h),
                waits=[("pe", mm1_done[idx - 1])], sem="w1d")
        w2_wd[idx] = castdma(
            lambda ei=eidx, h=half: w2buf[:].rearrange("p (k d2) -> p k d2", d2=D),
            lambda ei=eidx, h=half: w2_src(ei, h),
            waits=[("pe", attn_pe_done if idx == 0 else mm2_done[idx - 1])], sem="w2d")

        if half == 0 and eidx != "sh":
            c_wr = s.op("sync",
                        (lambda e_=eidx: lambda e: e.dma_start(
                            out=wrow0[:], in_=wT[e_:e_ + 1, :]))(),
                        waits=[("dve", wT_done), wrow_rel], inc="wr", inc_k=16)
            c_bc = {}
            wb_cp = {}
            for n in range(NT):
                bk = 4 + n % 2
                c_bc[n] = s.op("tensor",
                               (lambda n_, bk_: lambda e: e.matmul(
                                   p[bk_][:], onesb_sb[:],
                                   wrow0[:, 512 * n_:512 * (n_ + 1)],
                                   start=True, stop=True))(n, bk),
                               waits=[("wr", c_wr), bank_rel[bk], wb_rel], inc="pe")
                wb_cp[n] = s.op("scalar",
                                (lambda n_, bk_: lambda e: e.activation(
                                    wb[:, 512 * n_:512 * (n_ + 1)], p[bk_][:],
                                    AF.Identity))(n, bk),
                                waits=[("pe", c_bc[n])], inc="act")
                bank_rel[bk] = ("act", wb_cp[n])
            wb_done_act = s.c("act")
            wrow_rel = ("pe", c_bc[NT - 1])

        mm1_last_pe = 0
        for n in range(NT):
            for mm in range(NH):
                bank = mm % 2
                c_mm = 0
                for k in range(KD):
                    if idx == 0:
                        wsem = (f"w1b{k}", 16) if k >= 4 else ("w1d", w1_wd[0])
                        xwaits = [wsem, bank_rel[bank], ("dve", scan_done)]
                    else:
                        xwaits = [("w1d", w1_wd[idx]), bank_rel[bank]]
                    c_mm = s.op("tensor",
                                (lambda k_, mm_, n_, b_: lambda e: e.matmul(
                                    p[b_][:],
                                    w1buf[:, HALF * k_ + 128 * mm_:HALF * k_ + 128 * (mm_ + 1)],
                                    xnT[:, T * k_ + 512 * n_:T * k_ + 512 * (n_ + 1)],
                                    start=(k_ == 0), stop=(k_ == KD - 1)))(k, mm, n, bank),
                                waits=xwaits, inc="pe" if k == KD - 1 else None)
                mm1_last_pe = c_mm
                bcol = (E * 32 + 16 * half + mm) if eidx == "sh" else (32 * eidx + 16 * half + mm)
                gwaits = [("pe", c_mm)]
                if idx == 0:
                    gwaits.append(("dve", scan_done))
                gelu_cnt[(idx, n, mm)] = s.op(
                    "scalar",
                    (lambda mm_, b_, bc_: lambda e: e.activation(
                        h_sb[:, 512 * mm_:512 * (mm_ + 1)], p[b_][:], AF.Gelu,
                        bias=b1all[:, bc_:bc_ + 1]))(mm, bank, bcol),
                    waits=gwaits, inc="act")
                bank_rel[bank] = ("act", gelu_cnt[(idx, n, mm)])
            for d in range(ND):
                bank = 2 + d % 2
                c_mm = 0
                sh_tail = (eidx == "sh" and half == 1)
                for k in range(NH):
                    is_stop = (k == NH - 1) and not sh_tail
                    c_mm = s.op("tensor",
                                (lambda k_, d_, n_, b_, st_: lambda e: e.matmul(
                                    p[b_][:],
                                    w2buf[:, D * k_ + 128 * d_:D * k_ + 128 * (d_ + 1)],
                                    h_sb[:, 512 * k_:512 * (k_ + 1)],
                                    start=(k_ == 0), stop=st_))(k, d, n, bank, is_stop),
                                waits=[("w2d", w2_wd[idx]),
                                       ("act", gelu_cnt[(idx, n, k)]),
                                       bank_rel[bank]],
                                inc="pe" if is_stop else None)
                if sh_tail:
                    c_mm = s.op("tensor",
                                (lambda d_, n_, b_: lambda e: e.matmul(
                                    p[b_][:],
                                    b2sb[:, 128 * d_:128 * (d_ + 1)],
                                    wT[:, 512 * n_:512 * (n_ + 1)],
                                    start=False, stop=True))(d, n, bank),
                                waits=[("dve", wT_done)], inc="pe")
                mm2_done_grp = c_mm
                prev = acc_last_for_ntile[n]
                if eidx == "sh":
                    if half == 0:
                        c_ev = s.op("vector",
                                    (lambda d_, n_, b_: lambda e: e.tensor_add(
                                        acc[:, T * d_ + 512 * n_:T * d_ + 512 * (n_ + 1)],
                                        acc[:, T * d_ + 512 * n_:T * d_ + 512 * (n_ + 1)],
                                        p[b_][:]))(d, n, bank),
                                    waits=[("pe", mm2_done_grp), prev], inc="dve")
                    else:
                        c_ev = s.op("vector",
                                    (lambda d_, n_, b_: lambda e: e.scalar_tensor_tensor(
                                        acc[:, T * d_ + 512 * n_:T * d_ + 512 * (n_ + 1)],
                                        p[b_][:], bs2_sb[:, d_:d_ + 1],
                                        acc[:, T * d_ + 512 * n_:T * d_ + 512 * (n_ + 1)],
                                        ALU.add, ALU.add))(d, n, bank),
                                    waits=[("pe", mm2_done_grp), prev], inc="dve")
                    bank_rel[bank] = ("dve", c_ev)
                else:
                    c_t = s.op("vector",
                               (lambda d_, n_, b_: lambda e: e.tensor_mul(
                                   tmp_eo[:], p[b_][:],
                                   wb[:, 512 * n_:512 * (n_ + 1)]))(d, n, bank),
                               waits=[("pe", mm2_done_grp), ("act", wb_done_act),
                                      ("dve", last_tmp_read)],
                               inc="dve")
                    bank_rel[bank] = ("dve", c_t)
                    last_wb_read = c_t
                    c_ev = s.op("vector",
                                (lambda d_, n_: lambda e: e.tensor_add(
                                    acc[:, T * d_ + 512 * n_:T * d_ + 512 * (n_ + 1)],
                                    acc[:, T * d_ + 512 * n_:T * d_ + 512 * (n_ + 1)],
                                    tmp_eo[:]))(d, n),
                                waits=[prev, ("dve", c_t)], inc="dve")
                    last_tmp_read = c_ev
                acc_last_for_ntile[n] = ("dve", c_ev)
                acc_evac_nd[(n, d)] = ("dve", c_ev)
                if idx == n_idx - 1 and n >= 1:
                    emit_final_half(4 * (n - 1) + d // 2, d % 2, n - 1)
                if idx == n_idx - 1 and n == NT - 1 and d == 3:
                    for ti in range(4):
                        emit_final_half(4 * n + ti, 0, n)
            if idx == n_idx - 1 and n == NT - 1:
                for ti in range(4):
                    emit_final_half(4 * n + ti, 1, n)
        mm1_done[idx] = mm1_last_pe
        mm2_done[idx] = s.c("pe")
        if half == 1 and eidx != "sh":
            wb_rel = ("dve", last_wb_read)

    s.op("sync", lambda e: e.nop(),
         waits=[("ou0", s.c("ou0")), ("ou1", s.c("ou1")),
                ("ou2", s.c("ou2")), ("ou3", s.c("ou3"))])
    if debug:
        dbg_list = [("d_wT", wT, BF16, [E, T]),
                    ("d_wb", wb, BF16, [128, T]),
                    ("d_bgc", bgate_bc, F32, [128, E]),
                    ("d_xnT", xnT, BF16, [128, KD * T]),
                    ("d_acc", acc, F32, [128, KD * T])]
        dl = 0
        for nm, t_, dty, shp in dbg_list:
            od = nc.dram_tensor(nm, shp, dty, kind="ExternalOutput").ap()
            dl = s.op("sync",
                      (lambda od_, t2: lambda e: e.dma_start(out=od_[:, :], in_=t2[:]))(od, t_),
                      waits=[("ou0", s.c("ou0")), ("ou1", s.c("ou1")),
                             ("ou2", s.c("ou2")), ("ou3", s.c("ou3"))],
                      inc="wr", inc_k=16)
        s.op("sync", lambda e: e.nop(), waits=[("wr", dl)])

    with nc.Block() as block:
        s.emit(block, sems)

    n_instr = sum(len(v) for v in s.ops.values())
    ctx_moe.close()
    ctx.close()
    return nc, n_instr


def host_inputs(inputs):
    f = np.ascontiguousarray
    base = {
        "awg": f(inputs["L0_attn_Wg"]),
        "awv": f(inputs["L0_attn_Wv"]),
        "awd": f(inputs["L0_attn_Wd"]),
        "battn": f(np.stack([
            inputs["L0_attn_bg"].reshape(KD, 128).T,
            inputs["L0_attn_bv"].reshape(KD, 128).T,
            inputs["L0_attn_bd"].reshape(KD, 128).T])),
        "wmoet": f(inputs["L0_ffn_Wg"].T),
        "bgr1": f(inputs["L0_ffn_bg"].reshape(1, E)),
        "bgr2": f(inputs["L0_ffn_bias"].reshape(1, E)),
        "w1t": f(inputs["L0_ffn_W1"].transpose(0, 2, 1)),
        "b1p": f(inputs["L0_ffn_b1"].reshape(E, 32, 128).transpose(0, 2, 1)),
        "w2t": f(inputs["L0_ffn_W2"].transpose(0, 2, 1)),
        "b2n": f(inputs["L0_ffn_b2"]),
        "ws1t": f(inputs["L0_ffn_Ws1"].T),
        "bs1p": f(inputs["L0_ffn_bs1"].reshape(32, 128).T),
        "ws2t": f(inputs["L0_ffn_Ws2"].T),
        "bs2p": f(inputs["L0_ffn_bs2"].reshape(KD, 128).T),
        "identf": np.eye(128, dtype=np.float32),
        "onesb": np.ones((1, 128), dtype=ml_dtypes.bfloat16),
        "onesf": np.ones((1, 128), dtype=np.float32),
    }
    x = np.asarray(inputs["L0_attn_x"], dtype=np.float32)
    maps = []
    for c in range(8):
        m = dict(base)
        m["x"] = f(x[c])
        maps.append(m)
    return maps


def kernel(**inputs):
    nc, _ = build_program(nexp=8)
    in_maps = host_inputs(inputs)
    res = run_bass_kernel_spmd(nc, in_maps, core_ids=list(range(8)))
    return np.stack([r["out"] for r in res.results], axis=0)


if __name__ == "__main__":
    import sys
    nexp = int(sys.argv[1]) if len(sys.argv) > 1 else 1
    rng = np.random.default_rng(0)
    sc = 0.02
    inputs = {
        "L0_attn_x": rng.standard_normal((8, T, D)).astype(np.float32),
        "L0_attn_Wg": (rng.standard_normal((D, D)) / np.sqrt(D)).astype(np.float32),
        "L0_attn_bg": (rng.standard_normal(D) * 0.1).astype(np.float32),
        "L0_attn_Wv": (rng.standard_normal((D, D)) / np.sqrt(D)).astype(np.float32),
        "L0_attn_bv": (rng.standard_normal(D) * 0.1).astype(np.float32),
        "L0_attn_Wd": (rng.standard_normal((D, D)) / np.sqrt(D)).astype(np.float32),
        "L0_attn_bd": (rng.standard_normal(D) * 0.5 + 1).astype(np.float32),
        "L0_ffn_Wg": (rng.standard_normal((E, D)) * sc).astype(np.float32),
        "L0_ffn_bg": (rng.standard_normal(E) * 0.1).astype(np.float32),
        "L0_ffn_bias": (rng.standard_normal(E) * 0.1).astype(np.float32),
        "L0_ffn_W1": (rng.standard_normal((E, H, D)) * sc).astype(np.float32),
        "L0_ffn_b1": (rng.standard_normal((E, H)) * 0.05).astype(np.float32),
        "L0_ffn_W2": (rng.standard_normal((E, D, H)) * sc).astype(np.float32),
        "L0_ffn_b2": (rng.standard_normal((E, D)) * 0.05).astype(np.float32),
        "L0_ffn_Ws1": (rng.standard_normal((H, D)) * sc).astype(np.float32),
        "L0_ffn_bs1": (rng.standard_normal(H) * 0.05).astype(np.float32),
        "L0_ffn_Ws2": (rng.standard_normal((D, H)) * sc).astype(np.float32),
        "L0_ffn_bs2": (rng.standard_normal(D) * 0.05).astype(np.float32),
    }

    def np_ref(c, nexp):
        from scipy.special import erf
        x = inputs["L0_attn_x"][c].astype(np.float64)
        m = x.mean(1, keepdims=True); v = x.var(1, keepdims=True)
        xn = (x - m) / np.sqrt(v + 1e-5)
        g = xn @ inputs["L0_attn_Wg"] + inputs["L0_attn_bg"]
        vv = xn @ inputs["L0_attn_Wv"] + inputs["L0_attn_bv"]
        d = xn @ inputs["L0_attn_Wd"] + inputs["L0_attn_bd"]
        sig = lambda z: 1 / (1 + np.exp(-z))
        xs = sig(g) * np.tanh(vv)
        a = 0.001 + 0.998 * sig(d)
        h = np.zeros_like(xs); st = np.zeros(D)
        for t in range(T):
            st = a[t] * st + xs[t]
            h[t] = st
        gelu = lambda z: 0.5 * z * (1 + erf(z / np.sqrt(2)))
        scores = sig(xn @ inputs["L0_ffn_Wg"].T + inputs["L0_ffn_bg"] + inputs["L0_ffn_bias"])
        idx = np.argsort(-scores, 1, kind="stable")[:, :2]
        w = np.zeros_like(scores)
        np.put_along_axis(w, idx, np.take_along_axis(scores, idx, 1), 1)
        comb = np.zeros_like(xn)
        for e in range(nexp):
            he = gelu(xn @ inputs["L0_ffn_W1"][e].T + inputs["L0_ffn_b1"][e])
            eo = he @ inputs["L0_ffn_W2"][e].T
            comb += w[:, e:e + 1] * eo
        comb += w[:, :] @ inputs["L0_ffn_b2"]
        shared = gelu(xn @ inputs["L0_ffn_Ws1"].T + inputs["L0_ffn_bs1"]) @ \
            inputs["L0_ffn_Ws2"].T + inputs["L0_ffn_bs2"]
        return x + h + shared + comb

    nc, n_instr = build_program(nexp=nexp)
    print("instructions:", n_instr)
    in_maps = host_inputs(inputs)
    res = run_bass_kernel_spmd(nc, in_maps[:1], core_ids=[0])
    got = res.results[0]["out"]
    ref = np_ref(0, nexp)
    err = np.linalg.norm(got - ref) / np.linalg.norm(ref)
    print("rel l2:", err)
    rvr = float(np.mean((got - ref) ** 2) / np.mean(ref ** 2))
    print("rvr:", rvr)
